# version 04 (final)
# baseline (speedup 1.0000x reference)
import numpy as np
import ml_dtypes
import concourse.bass as bass
import concourse.mybir as mybir
from concourse.bass_utils import run_bass_kernel_spmd
from concourse import library_config

F32 = mybir.dt.float32
BF16 = mybir.dt.bfloat16
AF = mybir.ActivationFunctionType
ALU = mybir.AluOpType
AX = mybir.AxisListType.X
DTSIZE = {F32: 4, BF16: 2}

NCORES = 8
S_FULL = 8192
TOK = S_FULL // NCORES
H = 4096
DSSM = 8192
NH = 64
PDIM = 128
NDIM = 128
CONV_DIM = DSSM + 2 * NDIM
PROJ_DIM = DSSM + CONV_DIM + NH
KCONV = 4
QC = 128
NCHUNK = TOK // QC
EPS = 1e-5

DEBUG_STAGE = None

ENGINES = ("te", "vec", "sc", "gp", "sy")
QUEUES = {
    "te": ("tensor", 1), "vec": ("vector", 1), "sc": ("scalar", 1),
    "gp": ("gpsimd", 1), "sy": ("sync", 1),
    "dsy": ("sync", 16), "dgp": ("gpsimd", 16), "dsc": ("scalar", 16),
    "dcc": ("gpsimd", 1),
}
ENGINE_OF_Q = {"te": "te", "vec": "vec", "sc": "sc", "gp": "gp", "sy": "sy",
               "dsy": "sy", "dgp": "gp", "dsc": "sc", "dcc": "gp"}



class Res:
    __slots__ = ("name", "last_write", "readers", "reuse_deps")
    def __init__(self, name):
        self.name = name
        self.last_write = None
        self.readers = []
        self.reuse_deps = []


class Op:
    __slots__ = ("queue", "fn", "idx", "need_inc", "waits", "tag", "deps")
    def __init__(self, queue, fn, tag):
        self.queue = queue
        self.fn = fn
        self.idx = 0
        self.need_inc = False
        self.waits = []
        self.tag = tag
        self.deps = {}


class Prog:
    def __init__(self, nc):
        self.nc = nc
        self.ops = []
        self.res = {}
        self.drained = {}

    def R(self, obj):
        key = obj if isinstance(obj, (str, tuple)) else id(obj)
        r = self.res.get(key)
        if r is None:
            nm = obj if isinstance(obj, (str, tuple)) else getattr(obj, "name", str(key))
            r = self.res[key] = Res(str(nm))
        return r

    def emit(self, queue, fn, reads=(), writes=(), tag=""):
        myeng = ENGINE_OF_Q[queue]
        rres = [self.R(r) for r in reads]
        wres = [self.R(w) for w in writes]
        if queue in ("vec", "sc", "gp"):
            need = -1
            cand = []
            for r in rres:
                cand.append(r.last_write)
            for w in wres:
                cand.append(w.last_write)
                cand.extend(w.readers)
                cand.extend(w.reuse_deps)
            for i in cand:
                if i is not None and i > need and self.ops[i].queue == queue:
                    need = i
            if need >= self.drained.get(queue, -1) and need >= 0:
                dop = Op(queue, lambda e: e.drain(), "drain")
                self.ops.append(dop)
                self.drained[queue] = len(self.ops) - 1

        op = Op(queue, fn, tag)
        n = len(self.ops)

        def add(i):
            if i is None:
                return
            dep = self.ops[i]
            q = dep.queue
            if q == queue:
                return
            if q == myeng and q in ENGINES:
                return
            if i > op.deps.get(q, -1):
                op.deps[q] = i

        for r in rres:
            add(r.last_write)
        for w in wres:
            add(w.last_write)
            for rd in w.readers:
                add(rd)
            for d in w.reuse_deps:
                add(d)
            w.reuse_deps = []
        for r in rres:
            r.readers.append(n)
        for w in wres:
            w.last_write = n
            w.readers = []
        self.ops.append(op)
        return op

    def dma(self, q, out, in_, reads=(), writes=(), tag="", **kw):
        return self.emit(q, lambda e: e.dma_start(out=out, in_=in_, **kw),
                         reads=reads, writes=writes, tag=tag)

    def finalize(self, trunc=None):
        nc = self.nc
        ops = self.ops if trunc is None else self.ops[:trunc]
        for op in ops:
            for q, i in op.deps.items():
                ops[i].need_inc = True
        cnt = {q: 0 for q in QUEUES}
        for op in ops:
            if op.queue in ("dsy", "dgp", "dsc", "dcc"):
                op.need_inc = True
            if op.need_inc:
                cnt[op.queue] += 1
                op.idx = cnt[op.queue]
        waited = {e: {} for e in ENGINES}
        for op in ops:
            myeng = ENGINE_OF_Q[op.queue]
            w = waited[myeng]
            for q, i in sorted(op.deps.items()):
                v = ops[i].idx
                if v > w.get(q, 0):
                    w[q] = v
                    op.waits.append((q, v * QUEUES[q][1]))
        sems = {q: nc.alloc_semaphore(f"sem_{q}") for q in QUEUES if cnt[q]}
        self.sems, self.counts = sems, cnt
        per_engine = {"tensor": [], "vector": [], "scalar": [], "gpsimd": [], "sync": []}
        for op in ops:
            per_engine[QUEUES[op.queue][0]].append(op)

        def runner(op_list):
            def run(eng):
                for op in op_list:
                    for q, v in op.waits:
                        eng.wait_ge(sems[q], v)
                    inst = op.fn(eng)
                    if op.need_inc:
                        inst.then_inc(sems[op.queue], QUEUES[op.queue][1])
            return run

        with nc.Block() as block:
            block.tensor(runner(per_engine["tensor"]))
            block.vector(runner(per_engine["vector"]))
            block.scalar(runner(per_engine["scalar"]))
            block.gpsimd(runner(per_engine["gpsimd"]))
            block.sync(runner(per_engine["sync"]))


class Buf:
    __slots__ = ("t", "name", "pool", "off", "size")
    def __init__(self, t, name, pool, off, size):
        self.t, self.name, self.pool, self.off, self.size = t, name, pool, off, size
    def __getitem__(self, k):
        return self.t[k]
    def free(self):
        self.pool.free(self)


class SbufPool:
    def __init__(self, nc, prog):
        self.nc, self.prog = nc, prog
        self.base, self.top = nc.sbuf_base, nc.sbuf_top
        self.live = []
        self.dead = []
        self.n = 0

    def alloc(self, name, shape, dtype, align=64):
        assert shape[0] <= 128, shape
        free_elems = int(np.prod(shape[1:]))
        size = free_elems * DTSIZE[dtype]
        off = self._find(size, align)
        self.n += 1
        t = self.nc.alloc_sbuf_tensor_at(f"{name}_{self.n}", list(shape), dtype, offset=off)
        buf = Buf(t, name, self, off, size)
        deps, kept = [], []
        for (o, s, d) in self.dead:
            if o < off + size and off < o + s:
                deps.extend(d)
                if o < off:
                    kept.append((o, off - o, d))
                if o + s > off + size:
                    kept.append((off + size, o + s - (off + size), d))
            else:
                kept.append((o, s, d))
        self.dead = kept
        self.prog.R(buf).reuse_deps = deps
        self.live.append((off, size, buf))
        return buf

    def _find(self, size, align):
        occ = sorted([(o, s) for (o, s, _) in self.live])
        pos = -(-self.base // align) * align
        for (o, s) in occ:
            if pos + size <= o:
                break
            if pos < o + s:
                pos = max(pos, o + s)
                pos = -(-pos // align) * align
        assert pos + size <= self.top, (
            f"SBUF pool exhausted: need {size}B at {pos}, top {self.top}")
        return pos

    def free(self, buf):
        import os as _os
        if _os.environ.get("MK_NOFREE"):
            return
        for i, (o, s, b) in enumerate(self.live):
            if b is buf:
                self.live.pop(i)
                r = self.prog.res.get(id(buf))
                deps = []
                if r is not None:
                    if r.last_write is not None:
                        deps.append(r.last_write)
                    deps.extend(r.readers)
                self.dead.append((o, s, deps))
                return
        raise KeyError(f"free of non-live buf {buf.name}")

    def used_bytes(self):
        return sum(s for (_, s, _) in self.live)


class _BitcastView:
    def __init__(self, handle, dt):
        self.handle, self.dt = handle, dt
    def __getitem__(self, k):
        return self.handle[:, :].bitcast(self.dt)[k]


class PsumPool:
    def __init__(self, nc):
        self.banks = []
        for i in range(8):
            t = nc.alloc_psum_tensor(f"psb{i}", [128, 512], F32)
            self.banks.append(PsBank(t, _BitcastView(t, BF16), i))
        self.rr = 0

    def take(self):
        b = self.banks[self.rr % 8]
        self.rr += 1
        return b


class PsBank:
    __slots__ = ("f32", "bf16", "i", "name")
    def __init__(self, f32, bf16, i):
        self.f32, self.bf16, self.i, self.name = f32, bf16, i, f"psum{i}"



class C:
    def __init__(self, nc, debug_dump=None):
        self.nc = nc
        self.p = Prog(nc)
        self.pool = SbufPool(nc, self.p)
        self.ps = PsumPool(nc)
        self.io = {}
        self.dram = {}
        self.debug_dump = debug_dump or {}
        self.dbg = {}
        self.gp_lib_loaded = False

    def inp(self, name, shape, dtype=F32):
        t = self.nc.dram_tensor(name, list(shape), dtype, kind="ExternalInput")
        self.io[name] = t.ap()
        return self.io[name]

    def outp(self, name, shape, dtype=F32):
        t = self.nc.dram_tensor(name, list(shape), dtype, kind="ExternalOutput")
        self.io[name] = t.ap()
        return self.io[name]

    def scratch(self, name, shape, dtype):
        t = self.nc.dram_tensor(name, list(shape), dtype)
        self.dram[name] = t.ap()
        return self.dram[name]

    def copy(self, out_ap, in_ap, reads, writes, q="sc", tag="copy"):
        eng_fn = (lambda e: e.copy(out_ap, in_ap)) if q == "sc" else (lambda e: e.tensor_copy(out_ap, in_ap))
        return self.p.emit(q, eng_fn, reads=reads, writes=writes, tag=tag)

    def transpose(self, out_ap, in_ap, ident_ap, reads, writes, tag="tr"):
        return self.p.emit("te", lambda e: e.transpose(out_ap, in_ap, ident_ap),
                           reads=reads, writes=writes, tag=tag)

    def mm(self, out_ap, lhsT_ap, rhs_ap, start, stop, reads, writes, tag="mm"):
        return self.p.emit("te", lambda e: e.matmul(out_ap, lhsT_ap, rhs_ap, start=start, stop=stop),
                           reads=reads, writes=writes, tag=tag)

    def bcast(self, out_ap, row_ap, reads, writes, tag="bcast"):
        assert self.gp_lib_loaded
        return self.p.emit("gp", lambda e: e.partition_broadcast(out_ap, row_ap),
                           reads=reads, writes=writes, tag=tag)

    def dump(self, name, src_ap, reads):
        if name not in self.debug_dump:
            return
        shape, dtype = list(src_ap.shape), src_ap.dtype
        o = self.outp("dbg_" + name, shape, dtype)
        self.p.dma(("dsy"), o[tuple(slice(0, s) for s in shape)], src_ap, reads=reads,
                   writes=[o], tag="dump_" + name)
        self.dbg[name] = o


def build_consts(c):
    p, nc = c.p, c.nc
    c.eye_f32 = c.pool.alloc("eye32", [128, 128], F32)
    c.eye_bf = c.pool.alloc("eyebf", [128, 128], BF16)
    c.ones_bf = c.pool.alloc("onesbf", [128, 1], BF16)
    c.zero_col = c.pool.alloc("zerocol", [128, 1], F32)
    iot = c.pool.alloc("iot", [128, 128], F32)
    p.emit("gp", lambda e: e.iota(iot[:, :], [[1, 128]], channel_multiplier=-1,
                                  allow_small_or_imprecise_dtypes=True),
           writes=[iot], tag="iota")
    p.emit("vec", lambda e: e.tensor_scalar(c.eye_f32[:, :], iot[:, :], 0.0, None, ALU.is_equal),
           reads=[iot], writes=[c.eye_f32], tag="eye32")
    p.emit("vec", lambda e: e.tensor_copy(c.eye_bf[:, :], c.eye_f32[:, :]),
           reads=[c.eye_f32], writes=[c.eye_bf], tag="eyebf")
    p.emit("vec", lambda e: e.memset(c.ones_bf[:, :], 1.0), writes=[c.ones_bf])
    p.emit("vec", lambda e: e.memset(c.zero_col[:, :], 0.0), writes=[c.zero_col])
    c.eps_col = c.pool.alloc("epscol", [128, 1], F32)
    p.emit("vec", lambda e: e.memset(c.eps_col[:, :], EPS), writes=[c.eps_col])
    c.one_col = c.pool.alloc("onecol", [128, 1], F32)
    p.emit("vec", lambda e: e.memset(c.one_col[:, :], 1.0), writes=[c.one_col])
    c.m01t = c.pool.alloc("m01t", [128, 128], BF16)
    p.emit("vec", lambda e: e.tensor_scalar(c.m01t[:, :], iot[:, :], 0.0, None, ALU.is_ge),
           reads=[iot], writes=[c.m01t], tag="m01t")
    iot.free()


def load_cast(c, name, dram_view, shape, dtype=BF16, q=None, tag=None):
    cast = (dtype != dram_view.dtype)
    if q is None:
        q = "dgp" if cast else "dsy"
    if cast:
        assert q == "dgp", "dtype-casting DMA must use the gpsimd queue"
    b = c.pool.alloc(name, shape, dtype)
    c.p.dma(q, b[tuple(slice(0, s) for s in shape)], dram_view,
            reads=[], writes=[b], tag=tag or f"ld_{name}")
    return b


def load_transposed(c, name, dram_ap, rows, cols, keep_nat=False, out_dtype=BF16):
    p, ps = c.p, c.ps
    rc = max(1, rows // 128)
    if rows > 128:
        assert rows % 128 == 0
        nat = load_cast(c, name + "_nat", dram_ap.rearrange("(rc p) c -> p rc c", p=128),
                        [128, rc, cols], out_dtype,
                        q="dgp" if out_dtype == BF16 else "dsy")
        prow = 128
    else:
        nat = load_cast(c, name + "_nat",
                        dram_ap.unsqueeze(1) if dram_ap.ndim == 2 else dram_ap,
                        [rows, 1, cols], out_dtype,
                        q="dgp" if out_dtype == BF16 else "dsy")
        prow = rows
    cc = -(-cols // 128)
    out = c.pool.alloc(name + "_T", [min(cols, 128), cc, rows], out_dtype)
    eye = c.eye_bf if out_dtype == BF16 else c.eye_f32
    maxb = 8 if out_dtype == BF16 else 4
    bview = lambda b: (b.bf16 if out_dtype == BF16 else b.f32)
    for ci in range(cc):
        cw = min(cols - ci * 128, 128)
        ri = 0
        while ri < rc:
            nb = min(maxb, rc - ri) if prow == 128 else 1
            bank = ps.take()
            for j in range(nb):
                c.transpose(bview(bank)[0:cw, j * 128: j * 128 + prow],
                            nat[0:prow, ri + j, ci * 128: ci * 128 + cw],
                            eye[0:prow, 0:prow],
                            reads=[nat, eye], writes=[bank], tag="trw")
            p.emit("sc", (lambda dst, srcv: lambda e: e.copy(dst, srcv))(
                        out[0:cw, ci, ri * 128: ri * 128 + nb * prow],
                        bview(bank)[0:cw, 0: (nb - 1) * 128 + prow]),
                   reads=[bank], writes=[out], tag="trw_evac")
            ri += nb
    if keep_nat:
        return out, nat
    nat.free()
    return out



def ln_tile(c, x_ap, h_ap, sq, st, xreads, hwrites, dim=H, pr=128):
    p = c.p
    s = lambda j: st[0:pr, j:j + 1]
    p.emit("sc", lambda e: e.activation(sq[0:pr, 0:dim], x_ap, AF.Square,
                                        accum_out=s(0)),
           reads=xreads, writes=[sq, st], tag="ln_sq")
    p.emit("vec", lambda e: e.reduce_sum(s(1), x_ap, AX), reads=xreads, writes=[st], tag="ln_sum")
    p.emit("vec", lambda e: e.tensor_scalar(s(2), s(1), 1.0 / dim, None, ALU.mult),
           reads=[st], writes=[st], tag="ln_mu")
    p.emit("vec", lambda e: e.tensor_tensor(s(3), s(2), s(2), ALU.mult),
           reads=[st], writes=[st], tag="ln_mu2")
    p.emit("vec", lambda e: e.tensor_scalar(s(4), s(0), 1.0 / dim, None, ALU.mult),
           reads=[st], writes=[st], tag="ln_ex2")
    p.emit("vec", lambda e: e.tensor_sub(s(5), s(4), s(3)), reads=[st], writes=[st], tag="ln_var")
    p.emit("sc", lambda e: e.activation(s(6), s(5), AF.Ln, bias=c.eps_col[0:pr, 0:1]),
           reads=[st, c.eps_col], writes=[st], tag="ln_ln")
    p.emit("sc", lambda e: e.activation(s(7), s(6), AF.Exp, scale=-0.5),
           reads=[st], writes=[st], tag="ln_rs")
    p.emit("vec", lambda e: e.tensor_scalar(h_ap, x_ap, s(2), s(7), ALU.subtract, ALU.mult),
           reads=list(xreads) + [st], writes=hwrites, tag="ln_norm")


def transpose_tile_group(c, srcs, dst_ap, dst_res, reads, pr=128, cw=128):
    bank = c.ps.take()
    for j, s_ap in enumerate(srcs):
        c.transpose(bank.bf16[0:cw, j * 128: j * 128 + pr], s_ap, c.eye_bf[0:pr, 0:pr],
                    reads=list(reads) + [c.eye_bf], writes=[bank], tag="tr")
    n = len(srcs)
    c.p.emit("sc", lambda e: e.copy(dst_ap, bank.bf16[0:cw, 0:(n - 1) * 128 + pr]
                                    if n > 1 else bank.bf16[0:cw, 0:pr]),
             reads=[bank], writes=[dst_res], tag="tr_evac")


def tr_f32(c, src_ap, pr, nf, dst_ap, dst_res, reads):
    p = c.p
    hi = c.pool.alloc("trf_hi", [pr, nf], BF16)
    lo = c.pool.alloc("trf_lo", [pr, nf], BF16)
    p.emit("vec", lambda e: e.tensor_copy(hi[:, :], src_ap), reads=reads, writes=[hi],
           tag="trf_hi")
    p.emit("vec", lambda e: e.tensor_tensor(lo[:, :], src_ap, hi[:, :], ALU.subtract),
           reads=list(reads) + [hi], writes=[lo], tag="trf_lo")
    bank = c.ps.take()
    c.transpose(bank.bf16[0:nf, 0:pr], hi[:, :], c.eye_bf[0:pr, 0:pr],
                reads=[hi, c.eye_bf], writes=[bank], tag="trf_t1")
    c.transpose(bank.bf16[0:nf, 128:128 + pr], lo[:, :], c.eye_bf[0:pr, 0:pr],
                reads=[lo, c.eye_bf], writes=[bank], tag="trf_t2")
    p.emit("sc", lambda e: e.copy(dst_ap, bank.bf16[0:nf, 0:pr]),
           reads=[bank], writes=[dst_res], tag="trf_hi_ev")
    p.emit("vec", lambda e: e.tensor_tensor(dst_ap, dst_ap, bank.bf16[0:nf, 128:128 + pr],
               ALU.add), reads=[bank, dst_res], writes=[dst_res], tag="trf_comb")
    hi.free()
    lo.free()



def phase_a0(c):
    p, pool, ps = c.p, c.pool, c.ps
    x_in, xh_in = c.io["x"], c.io["x_halo"]
    c.d_x0 = c.scratch("d_x0", [8, 128, H], BF16)
    lnx = c.lnx = pool.alloc("lnx_fm", [128, 32, TOK + 3], BF16)
    sq = pool.alloc("sq_scr", [128, H], BF16)
    st = pool.alloc("ln_st", [128, 8], F32)
    xf = [pool.alloc(f"a0xf{i}", [128, H], F32) for i in range(2)]
    hb = [pool.alloc(f"a0hb{i}", [128, H], BF16) for i in range(2)]
    xb = [pool.alloc(f"a0xb{i}", [128, H], BF16) for i in range(2)]
    for tc in range(8):
        xt, hbt, xbt = xf[tc % 2], hb[tc % 2], xb[tc % 2]
        p.dma("dsy", xt[:, :], x_in[tc * 128:(tc + 1) * 128, :], writes=[xt], tag="ldx")
        ln_tile(c, xt[:, :], hbt[:, :], sq, st, xreads=[xt], hwrites=[hbt])
        p.emit("vec", lambda e, a=xbt, b=xt: e.tensor_copy(a[:, :], b[:, :]),
               reads=[xt], writes=[xbt], tag="castx")
        p.dma("dsy", c.d_x0[tc, :, :], xbt[:, :], reads=[xbt], writes=[c.d_x0], tag="spx")
        for g in range(4):
            srcs = [hbt[:, (g * 8 + j) * 128:(g * 8 + j + 1) * 128] for j in range(8)]
            dst = lnx.t[:, g * 8:(g + 1) * 8, 3 + tc * 128:3 + (tc + 1) * 128]
            transpose_tile_group(c, srcs, dst, lnx, reads=[hbt])
    xh = pool.alloc("a0xh", [3, H], F32)
    hh = pool.alloc("a0hh", [3, H], BF16)
    p.dma("dsy", xh[:, :], xh_in[:, :], writes=[xh], tag="ldxh")
    ln_tile(c, xh[:, :], hh[:, :], sq, st, xreads=[xh], hwrites=[hh], pr=3)
    for g in range(4):
        bank = c.ps.take()
        for j in range(8):
            fc = g * 8 + j
            c.transpose(bank.bf16[0:128, j * 128:j * 128 + 3], hh[0:3, fc * 128:(fc + 1) * 128],
                        c.eye_bf[0:3, 0:3], reads=[hh, c.eye_bf], writes=[bank], tag="trh")
        dst = lnx.t[:, g * 8:(g + 1) * 8, 0:3]
        src = bank.bf16[:, :].rearrange("p (j t) -> p j t", j=8)[:, :, 0:3]
        p.emit("sc", lambda e, d=dst, s2=src: e.copy(d, s2), reads=[bank], writes=[lnx], tag="trh_ev")
    for b in xf + hb + xb + [xh, hh, sq, st]:
        b.free()
    return lnx



WEIGHT_SPECS = {
    "L0_attn_W_in": (H, PROJ_DIM), "L0_attn_conv_w": (CONV_DIM, KCONV),
    "L0_attn_dt_bias": (NH,), "L0_attn_A_log": (NH,), "L0_attn_D_param": (NH,),
    "L0_attn_norm_weight": (DSSM,), "L0_attn_W_out": (DSSM, H),
    "L0_ffn_Wq": (64, H), "L0_ffn_Wr": (48, 64), "L0_ffn_aux_keys": (1024, 64),
    "L0_ffn_aux_values": (1024, H), "L0_ffn_Wvd": (256, H), "L0_ffn_Wvu": (H, 256),
    "L0_ffn_Wu1": (1024, H), "L0_ffn_bu1": (1024,), "L0_ffn_Wu2": (1, 1024),
    "L0_ffn_bu2": (1,), "L0_ffn_gate_w1": (1,), "L0_ffn_gate_bias": (1,),
    "L0_ffn_log_rel": (1024,),
    "L1_attn_Wqkv": (H, 3072), "L1_attn_bqkv": (3072,), "L1_attn_Wo": (1024, H),
    "L1_attn_bo": (H,),
    "L1_ffn_A": (64, 16, H), "L1_ffn_B": (64, H, 16), "L1_ffn_Wr": (64, H),
}


def build_module(debug_dump=None, stage=4):
    nc = bass.Bass("TRN2", target_bir_lowering=False, detect_race_conditions=False)
    c = C(nc, debug_dump)
    c.stage = stage
    c.inp("x", [TOK, H])
    c.inp("x_halo", [3, H])
    c.inp("prevmask", [1, NCORES])
    for nm, shp in WEIGHT_SPECS.items():
        c.inp(nm, list(shp) if len(shp) > 1 else [1, shp[0]])
    c.out_ap = c.outp("out", [TOK, H])
    build_consts(c)
    phase_a0(c)
    c.dump("lnx", c.lnx.t[:, :, :], reads=[c.lnx])
    if getattr(c, "stop_after_a0", False) or stage == 0:
        pass
    else:
        l0_attn(c)
        if stage >= 2:
            l0_ffn(c)
        if stage >= 3:
            l1_attn(c)
        if stage >= 4:
            l1_ffn(c)
    finish(c)
    import os as _os
    _tr = _os.environ.get("MK_TRUNC")
    c.p.finalize(trunc=int(_tr) if _tr else None)
    return nc, c


def finish(c):
    p = c.p
    if getattr(c, "out_written", False):
        pass
    elif getattr(c, "final_dram", None):
        xb = c.pool.alloc("finx", [128, H], BF16)
        fb = c.pool.alloc("finf", [128, H], F32)
        for tc in range(8):
            p.dma("dsy", xb[:, :], c.dram[c.final_dram][tc], reads=[c.final_dram],
                  writes=[xb], tag="fin_ld")
            p.emit("vec", lambda e: e.tensor_copy(fb[:, :], xb[:, :]),
                   reads=[xb], writes=[fb], tag="fin_cast")
            p.dma("dsy", c.out_ap[tc * 128:(tc + 1) * 128, :], fb[:, :],
                  reads=[fb], writes=[c.out_ap], tag="fin_out")
    elif not hasattr(c, "x_cur") or c.x_cur is None:
        z8 = c.pool.alloc("zout", [128, H], F32)
        p.emit("vec", lambda e: e.memset(z8[:, :], 0.0), writes=[z8])
        for tc in range(8):
            p.dma("dsy", c.out_ap[tc * 128:(tc + 1) * 128, :], z8[:, :],
                  reads=[z8], writes=[c.out_ap], tag="fin_out0")
    else:
        xb = c.x_cur
        tmp = c.pool.alloc("fin32", [128, 8, H], F32)
        p.emit("vec", lambda e: e.tensor_copy(tmp[:, :, :], xb[:, :, :]),
               reads=[xb], writes=[tmp], tag="fin_cast")
        for tc in range(8):
            p.dma("dsy", c.out_ap[tc * 128:(tc + 1) * 128, :], tmp[:, tc, :],
                  reads=[tmp], writes=[c.out_ap], tag="fin_out")
    p.emit("sy", lambda e: e.nop(), reads=[c.out_ap] + [c.io[k] for k in c.io
           if k.startswith("dbg_")], tag="final_wait")


def kernel(**inputs):
    nc, c = build_module(stage=4 if DEBUG_STAGE is None else DEBUG_STAGE)
    in_maps = _in_maps(inputs)
    res = run_bass_kernel_spmd(nc, in_maps, core_ids=list(range(NCORES)))
    out = np.concatenate([res.results[i]["out"][None] for i in range(NCORES)], axis=0)
    return out.reshape(1, S_FULL, H).astype(np.float32)


def _in_maps(inputs):
    x = np.asarray(inputs["L0_attn_hidden_states"]).reshape(S_FULL, H)
    maps = []
    for i in range(NCORES):
        m = {}
        m["x"] = np.ascontiguousarray(x[i * TOK:(i + 1) * TOK])
        m["x_halo"] = (np.zeros((3, H), np.float32) if i == 0
                       else np.ascontiguousarray(x[i * TOK - 3:i * TOK]))
        m["prevmask"] = (np.arange(NCORES) < i).astype(np.float32).reshape(1, NCORES)
        for nm, shp in WEIGHT_SPECS.items():
            a = np.asarray(inputs[nm])
            if a.ndim <= 1:
                a = a.reshape(1, -1)
            m[nm] = np.ascontiguousarray(a)
        maps.append(m)
    return maps



def l0_attn(c):
    p, pool, ps = c.p, c.pool, c.ps
    convw = load_cast(c, "convw", c.io["L0_attn_conv_w"].rearrange("(pc q) j -> q pc j", q=128),
                      [128, 66, KCONV], F32, q="dsy")
    dtb = load_cast(c, "dtb", c.io["L0_attn_dt_bias"].rearrange("a b -> (a b)").unsqueeze(1),
                    [NH, 1], F32, q="dsy")
    alog = load_cast(c, "alog", c.io["L0_attn_A_log"].rearrange("a b -> (a b)").unsqueeze(1),
                     [NH, 1], F32, q="dsy")
    na = pool.alloc("na", [NH, 1], F32)
    p.emit("sc", lambda e: e.activation(na[:, :], alog[:, :], AF.Exp, scale=1.0),
           reads=[alog], writes=[na], tag="expA")
    p.emit("vec", lambda e: e.tensor_scalar(na[:, :], na[:, :], -1.0, None, ALU.mult),
           reads=[na], writes=[na], tag="negA")
    dbc = load_cast(c, "dbc", c.io["L0_attn_D_param"].broadcast_to([128, NH]),
                    [128, NH], BF16, q="dgp")
    nw_nat = load_cast(c, "nw_nat", c.io["L0_attn_norm_weight"]
                       .rearrange("a (f q) -> (a f) q", q=128), [64, 128], F32, q="dsy")
    nwT = pool.alloc("nw_T", [128, 1, NH], F32)
    tr_f32(c, nw_nat[:, :], 64, 128, nwT[:, 0, :], nwT, reads=[nw_nat])
    nw_nat.free()
    d_diag = c.scratch("d_diag", [66, KCONV, 128, 128], BF16)
    diag_b = pool.alloc("diagb", [128, 66, 128], BF16)
    for j in range(KCONV):
        p.emit("vec", lambda e, jj=j: e.tensor_tensor(
                   diag_b[:, :, :],
                   c.eye_bf[:, :].unsqueeze(1).broadcast_to([128, 66, 128]),
                   convw[:, :, jj:jj + 1].broadcast_to([128, 66, 128]),
                   ALU.mult),
               reads=[c.eye_bf, convw], writes=[diag_b], tag="diag")
        p.dma("dsy", c.dram["d_diag"][:, j, :, :].transpose([1, 0, 2]), diag_b[:, :, :],
              reads=[diag_b], writes=["d_diag"], tag="st_diag")
    diag_b.free()

    c.d_xbc = c.scratch("d_xbc", [66, 128, TOK + 3], BF16)
    dtraw = pool.alloc("dtraw", [NH, TOK + 3], F32)
    wview = c.io["L0_attn_W_in"].rearrange("(kc q) m -> q kc m", q=128)
    wring = [pool.alloc(f"winr{i}", [128, 32, 256], BF16) for i in range(3)]
    stg = [pool.alloc(f"xstg{i}", [128, TOK + 3], BF16) for i in range(3)]
    ntiles = [(0, 512), (512, 512), (1024, 3)]
    slabs = [(DSSM + 256 * i, 256) for i in range(33)] + [(16640, 64)]
    sgi = 0
    for si, (m0, mw) in enumerate(slabs):
        wb = wring[si % 3]
        p.dma("dgp", wb[:, :, 0:mw], wview[:, :, m0:m0 + mw], writes=[wb], tag="ldWinX")
        if si == 0:
            c.dump("wb0", wb[:, :, :], reads=[wb])
        for mi in range(-(-mw // 128)):
            mcw = min(128, mw - mi * 128)
            mg = m0 + mi * 128
            is_dt = mg >= DSSM + CONV_DIM
            sb = None
            if not is_dt:
                sb = stg[sgi % 3]; sgi += 1
            for (n0, ncw) in ntiles:
                bank = ps.take()
                for kc in range(32):
                    c.mm(bank.f32[0:mcw, 0:ncw], wb[:, kc, mi * 128:mi * 128 + mcw],
                         c.lnx[:, kc, n0:n0 + ncw], start=(kc == 0), stop=(kc == 31),
                         reads=[wb, c.lnx], writes=[bank], tag="projX")
                if is_dt:
                    p.emit("sc", lambda e, b=bank, n0=n0, ncw=ncw:
                           e.copy(dtraw[0:NH, n0:n0 + ncw], b.f32[0:NH, 0:ncw]),
                           reads=[bank], writes=[dtraw], tag="dt_evac")
                else:
                    p.emit("sc", lambda e, b=bank, sb=sb, n0=n0, ncw=ncw:
                           e.copy(sb[:, n0:n0 + ncw], b.f32[0:128, 0:ncw]),
                           reads=[bank], writes=[sb], tag="xbc_evac")
            if not is_dt:
                pc = (mg - DSSM) // 128
                p.dma("dsy", c.dram["d_xbc"][pc], sb[:, :], reads=[sb],
                      writes=["d_xbc"], tag="st_xbc")
    for b in stg + wring:
        b.free()

    dt = pool.alloc("dt", [NH, TOK], F32)
    loga = pool.alloc("loga", [NH, TOK], F32)
    lndt = pool.alloc("lndt", [NH, TOK], F32)
    acum = pool.alloc("acum", [NH, NCHUNK, QC], F32)
    aend = pool.alloc("aend", [NH, NCHUNK], F32)
    logd_tot = pool.alloc("logdtot", [NH, 1], F32)
    sp1 = pool.alloc("sp1", [NH, TOK], F32)
    p.emit("sc", lambda e: e.activation(sp1[:, :], dtraw[0:NH, 3:TOK + 3], AF.Abs,
                                        bias=dtb[:, 0:1]),
           reads=[dtraw, dtb], writes=[sp1], tag="sp_abs")
    p.emit("sc", lambda e: e.activation(sp1[:, :], sp1[:, :], AF.Exp, scale=-1.0),
           reads=[sp1], writes=[sp1], tag="sp_exp")
    p.emit("sc", lambda e: e.activation(sp1[:, :], sp1[:, :], AF.Ln,
                                        bias=c.one_col[0:NH, 0:1]),
           reads=[sp1, c.one_col], writes=[sp1], tag="sp_ln")
    p.emit("sc", lambda e: e.activation(dt[:, :], dtraw[0:NH, 3:TOK + 3], AF.Relu,
                                        bias=dtb[:, 0:1]),
           reads=[dtraw, dtb], writes=[dt], tag="sp_relu")
    p.emit("vec", lambda e: e.tensor_add(dt[:, :], dt[:, :], sp1[:, :]),
           reads=[dt, sp1], writes=[dt], tag="sp_add")
    sp1.free()
    p.emit("vec", lambda e: e.tensor_scalar(loga[:, :], dt[:, :], na[:, 0:1], None, ALU.mult),
           reads=[dt, na], writes=[loga], tag="loga")
    p.emit("sc", lambda e: e.activation(lndt[:, :], dt[:, :], AF.Ln),
           reads=[dt], writes=[lndt], tag="lndt")
    zv = c.zero_col[0:NH, 0:1].broadcast_to([NH, QC])
    for ci in range(NCHUNK):
        p.emit("vec", lambda e, ci=ci: e.tensor_tensor_scan(
                   acum[:, ci, :], loga[:, ci * QC:(ci + 1) * QC], zv, 0.0, ALU.add, ALU.add),
               reads=[loga, c.zero_col], writes=[acum], tag="acum")
        p.emit("vec", lambda e, ci=ci: e.tensor_copy(aend[:, ci:ci + 1], acum[:, ci, QC - 1:QC]),
               reads=[acum], writes=[aend], tag="aend")
    p.emit("vec", lambda e: e.reduce_sum(logd_tot[:, 0:1], aend[:, :], AX),
           reads=[aend], writes=[logd_tot], tag="logdtot")
    c.d_acum = c.scratch("d_acum", [NCHUNK, NH, QC], F32)
    for ci in range(NCHUNK):
        p.dma("dsy", c.dram["d_acum"][ci], acum[0:NH, ci, :], reads=[acum],
              writes=["d_acum"], tag="st_acum")
    acumT = pool.alloc("acumT", [128, NCHUNK, NH], F32)
    lndtT = pool.alloc("lndtT", [128, NCHUNK, NH], F32)
    for ci in range(NCHUNK):
        tr_f32(c, acum[0:NH, ci, :], NH, QC, acumT[:, ci, :], acumT, reads=[acum])
        tr_f32(c, lndt[0:NH, ci * QC:(ci + 1) * QC], NH, QC, lndtT[:, ci, :], lndtT,
               reads=[lndt])
    colv = pool.alloc("colv", [128, NCHUNK, NH], F32)
    p.emit("vec", lambda e: e.tensor_sub(colv[:, :, :], lndtT[:, :, :], acumT[:, :, :]),
           reads=[lndtT, acumT], writes=[colv], tag="colv")
    lndtT.free()
    dtraw.free()
    convw.free()
    c.mamba = M = dict(dt=dt, loga=loga, lndt=lndt, acum=acum, aend=aend, logd=logd_tot,
                   acumT=acumT, colv=colv, convw=convw, dbc=dbc, nwT=nwT)
    c.dump("dt", dt[:, 0:TOK], reads=[dt])
    c.dump("acum", acum[:, :, :], reads=[acum])
    for b in (dt, loga, lndt, acum, aend, acumT):
        b.free()
    c.dump("dxbc", c.dram["d_xbc"][0:4, :, 0:131], reads=["d_xbc"])
    zbufs = l0a_proj_z(c)
    for b in zbufs:
        b.free()
    l0a_scan(c)
    l0a_state_exchange(c)
    l0a_pass2(c)
    M["logd"].free(); M["colv"].free(); M["dbc"].free()
    dtb.free(); alog.free(); na.free()
    l0a_out(c)
    c.mamba["nwT"].free()
    c.x_cur = sandwich_add(c, "d_att", "d_x0", "x1")
    c.dump("x1", c.x_cur[:, 0:2, :], reads=[c.x_cur])


def l0a_scan(c):
    p, pool, ps = c.p, c.pool, c.ps
    M = c.mamba
    c.d_y = c.scratch("d_y", [64, 128, TOK], BF16)
    c.d_sloc = c.scratch("d_sloc", [NCHUNK, 128, NH * 128], BF16)
    s_run = pool.alloc("s_run", [128, NH, 128], BF16)
    p.emit("vec", lambda e: e.memset(s_run[:, :, :], 0.0), writes=[s_run], tag="srun0")
    c.c_convs = pool.alloc("c_convs", [128, NCHUNK, QC], BF16)
    aendB = pool.alloc("aendB", [128, NCHUNK, NH], F32)
    dtotB = pool.alloc("dtotB", [128, NCHUNK, NH], BF16)
    for ci in range(NCHUNK):
        src = c.dram["d_acum"][ci, :, QC - 1:QC].rearrange("h o -> (h o)").unsqueeze(0)
        p.dma("dsy", aendB[:, ci, :], src.broadcast_to([128, NH]),
              reads=["d_acum"], writes=[aendB], tag="aendB",
              allow_slow_non_contiguous=True)
    p.emit("sc", lambda e: e.activation(dtotB[:, :, :], aendB[:, :, :], AF.Exp),
           reads=[aendB], writes=[dtotB], tag="dtotB")
    xbc = [pool.alloc("xbch", [128, 34, QC + 3], BF16)] * 2
    conv = [pool.alloc("convh", [128, 34, QC], BF16)] * 2
    diag = [pool.alloc("diagg", [128, 9, KCONV, 128], BF16)] * 2
    btok = pool.alloc("btok", [128, QC], BF16)
    bcA = pool.alloc("bcA", [128, 32 * QC], F32)
    expB = pool.alloc("expB", [128, 32, QC], BF16)
    smt = pool.alloc("smt", [128, 32, QC], BF16)
    cexp = pool.alloc("cexp", [128, 32, QC], BF16)
    gmt = pool.alloc("gmt", [128, QC], BF16)
    xtok = pool.alloc("xtok", [128, 32, 128], BF16)
    xw = pool.alloc("xw", [128, 32, 128], BF16)
    wlog = pool.alloc("wlog", [128, 32], F32)
    wexp = pool.alloc("wexp", [128, 32], F32)
    ystg = pool.alloc("ystg", [128, 32, QC], BF16)
    slocs = pool.alloc("slocs", [128, 32, 128], BF16)
    gctr = 0

    def EV(fn, reads, writes, q="vec", tag=""):
        c.p.emit(q, fn, reads=reads, writes=writes, tag=tag)

    bcA3 = bcA[:, :].rearrange("q (h t) -> q h t", h=32)
    for ci in range(NCHUNK):
        t0 = ci * QC
        for hf in range(2):
            hc0 = hf * 32
            xb, cv = xbc[hf], conv[hf]
            pcs = list(range(hc0, hc0 + 32)) + ([64, 65] if hf == 0 else [])
            p.dma("dsy", xb[:, 0:32, :],
                  c.dram["d_xbc"][hc0:hc0 + 32, :, t0:t0 + QC + 3].transpose([1, 0, 2]),
                  reads=["d_xbc"], writes=[xb], tag="ld_xbch")
            if hf == 0:
                p.dma("dsy", xb[:, 32:34, :],
                      c.dram["d_xbc"][64:66, :, t0:t0 + QC + 3].transpose([1, 0, 2]),
                      reads=["d_xbc"], writes=[xb], tag="ld_xbcbc")
            if ci == 0 and hf == 0:
                c.dump("xb0", xb[:, 0:34, :], reads=[xb])
            npc = len(pcs)
            for g in range(-(-npc // 8)):
                gw = min(8, npc - g * 8)
                pc0 = pcs[g * 8]
                dgr = diag[gctr % 2]; gctr += 1
                p.dma("dsy", dgr[:, 0:gw, :, :],
                      c.dram["d_diag"][pc0:pc0 + gw].transpose([2, 0, 1, 3]),
                      reads=["d_diag"], writes=[dgr], tag="ld_diag")
                if ci == 0 and hf == 0 and g == 0:
                    c.dump("dg0", dgr[:, 0:8, :, :], reads=[dgr])
                for i in range(gw):
                    bank = ps.take()
                    for j in range(KCONV):
                        c.mm(bank.f32[0:128, 0:QC], dgr[:, i, j, :],
                             xb[:, g * 8 + i, j:j + QC], start=(j == 0), stop=(j == KCONV - 1),
                             reads=[dgr, xb], writes=[bank], tag="convmm")
                    EV((lambda b, d2, i2: lambda e: e.activation(d2[:, i2, :],
                        b.f32[0:128, 0:QC], AF.Silu))(bank, cv, g * 8 + i),
                       reads=[bank], writes=[cv], q="sc", tag="convsilu")
            if hf == 0:
                bank = ps.take()
                c.transpose(bank.bf16[0:QC, 0:128], cv[:, 32, :], c.eye_bf[:, :],
                            reads=[cv, c.eye_bf], writes=[bank], tag="trB")
                EV((lambda b: lambda e: e.copy(btok[:, :], b.bf16[0:QC, 0:128]))(bank),
                   reads=[bank], writes=[btok], q="sc", tag="trB_ev")
                EV((lambda cv2, ci2: lambda e: e.tensor_copy(c.c_convs[:, ci2, :],
                    cv2[:, 33, :]))(cv, ci), reads=[cv], writes=[c.c_convs], tag="ccopy")
                bank2 = ps.take()
                c.mm(bank2.f32[0:QC, 0:QC], cv[:, 32, :], cv[:, 33, :], True, True,
                     reads=[cv], writes=[bank2], tag="gcbt")
                EV((lambda b: lambda e: e.tensor_tensor(gmt[:, :], b.f32[0:QC, 0:QC],
                    c.m01t[:, :], ALU.mult))(bank2),
                   reads=[bank2, c.m01t], writes=[gmt], tag="gmt")
            rowsrc = c.dram["d_acum"][ci, hc0:hc0 + 32, :].rearrange("h t -> (h t)").unsqueeze(0)
            p.dma("dsy", bcA[:, :], rowsrc.broadcast_to([128, 32 * QC]),
                  reads=["d_acum"], writes=[bcA], tag="bcA")
            EV(lambda e: e.activation(expB[:, :, :], bcA3, AF.Exp),
               reads=[bcA], writes=[expB], q="sc", tag="expB")
            EV((lambda ci2: lambda e: e.tensor_tensor(cexp[:, :, :],
                c.c_convs[:, ci2, :].unsqueeze(1).broadcast_to([128, 32, QC]),
                expB[:, :, :], ALU.mult))(ci),
               reads=[c.c_convs, expB], writes=[cexp], tag="cexp")
            EV((lambda ci2, h0: lambda e: e.tensor_tensor(bcA3, bcA3,
                M["colv"][:, ci2, h0:h0 + 32].unsqueeze(2).broadcast_to([128, 32, QC]),
                ALU.add))(ci, hc0), reads=[bcA, M["colv"]], writes=[bcA], tag="dif")
            EV(lambda e: e.tensor_scalar(bcA3, bcA3, 30.0, None, ALU.min),
               reads=[bcA], writes=[bcA], tag="difclamp")
            EV(lambda e: e.activation(smt[:, :, :], bcA3, AF.Exp),
               reads=[bcA], writes=[smt], q="sc", tag="smtexp")
            EV(lambda e: e.tensor_tensor(smt[:, :, :], smt[:, :, :],
               gmt[:, :].unsqueeze(1).broadcast_to([128, 32, QC]), ALU.mult),
               reads=[smt, gmt], writes=[smt], tag="smtg")
            if ci == 0 and hf == 0:
                c.dump("cv0", cv[:, 0:34, :], reads=[cv])
                c.dump("smt0", smt[:, :, :], reads=[smt])
                c.dump("gmt0", gmt[:, :], reads=[gmt])
                c.dump("cexp0", cexp[:, :, :], reads=[cexp])
            for g in range(4):
                bank = ps.take()
                for j in range(8):
                    c.transpose(bank.bf16[0:QC, j * 128:(j + 1) * 128], cv[:, g * 8 + j, :],
                                c.eye_bf[:, :], reads=[cv, c.eye_bf], writes=[bank], tag="trX")
                EV((lambda b, g2: lambda e: e.copy(xtok[:, g2 * 8:(g2 + 1) * 8, :],
                    b.bf16[:, 0:1024]))(bank, g),
                   reads=[bank], writes=[xtok], q="sc", tag="trX_ev")
            EV((lambda ci2, h0: lambda e: e.tensor_tensor(wlog[:, :],
                M["colv"][:, ci2, h0:h0 + 32], aendB[:, ci2, h0:h0 + 32], ALU.add))(ci, hc0),
               reads=[M["colv"], aendB], writes=[wlog], tag="wlog")
            EV(lambda e: e.activation(wexp[:, :], wlog[:, :], AF.Exp),
               reads=[wlog], writes=[wexp], q="sc", tag="wexp")
            EV(lambda e: e.tensor_tensor(xw[:, :, :], xtok[:, :, :],
               wexp[:, :].unsqueeze(2).broadcast_to([128, 32, 128]), ALU.mult),
               reads=[xtok, wexp], writes=[xw], tag="xwm")
            if ci == 0 and hf == 0:
                c.dump("xtok0", xtok[:, :, :], reads=[xtok])
                c.dump("xw0", xw[:, :, :], reads=[xw])
            banks = [ps.take() for _ in range(8)]
            for h in range(32):
                b = banks[h // 4]
                c.mm(b.f32[0:128, (h % 4) * 128:(h % 4 + 1) * 128], btok[:, :], xw[:, h, :],
                     True, True, reads=[btok, xw], writes=[b], tag="slocmm")
            for bi, b in enumerate(banks):
                EV((lambda b2, bi2: lambda e: e.copy(slocs[:, bi2 * 4:(bi2 + 1) * 4, :],
                    b2.f32[0:128, :]))(b, bi), reads=[b], writes=[slocs], q="sc", tag="sloc_ev")
            p.dma("dsy", c.dram["d_sloc"][ci, :, hc0 * 128:(hc0 + 32) * 128]
                  .rearrange("q (h t) -> q h t", h=32), slocs[:, :, :],
                  reads=[slocs], writes=["d_sloc"], tag="st_sloc")
            EV((lambda ci2, h0: lambda e: e.tensor_tensor(s_run[:, h0:h0 + 32, :],
                s_run[:, h0:h0 + 32, :],
                dtotB[:, ci2, h0:h0 + 32].unsqueeze(2).broadcast_to([128, 32, 128]),
                ALU.mult))(ci, hc0), reads=[s_run, dtotB], writes=[s_run], tag="srun_m")
            EV((lambda h0: lambda e: e.tensor_add(s_run[:, h0:h0 + 32, :],
                s_run[:, h0:h0 + 32, :], slocs[:, :, :]))(hc0),
               reads=[s_run, slocs], writes=[s_run], tag="srun_a")
            EV((lambda cv2, h0: lambda e: e.tensor_tensor(cv2[:, 0:32, :], cv2[:, 0:32, :],
                M["dbc"][:, h0:h0 + 32].unsqueeze(2).broadcast_to([128, 32, QC]),
                ALU.mult))(cv, hc0), reads=[cv, M["dbc"]], writes=[cv], tag="dx")
            banks = [ps.take() for _ in range(8)]
            for h in range(32):
                b = banks[h // 4]
                c.mm(b.f32[0:128, (h % 4) * 128:(h % 4 + 1) * 128], xtok[:, h, :], smt[:, h, :],
                     True, True, reads=[xtok, smt], writes=[b], tag="ymm")
            for bi, b in enumerate(banks):
                EV((lambda b2, bi2, cv2: lambda e: e.tensor_tensor(
                    ystg[:, bi2 * 4:(bi2 + 1) * 4, :], b2.f32[0:128, :],
                    cv2[:, bi2 * 4:(bi2 + 1) * 4, :], ALU.add))(b, bi, cv),
                   reads=[b, cv], writes=[ystg], tag="y_ev")
            p.dma("dsy", c.dram["d_y"][hc0:hc0 + 32, :, t0:t0 + QC].transpose([1, 0, 2]),
                  ystg[:, :, :], reads=[ystg], writes=["d_y"], tag="st_y")
    for b in [xbc[0], conv[0], diag[0], btok, bcA, expB, smt, cexp, gmt, xtok, xw, wlog,
              wexp, ystg, slocs]:
        b.free()
    c.s_run, c.aendB, c.dtotB = s_run, aendB, dtotB
    c.dump("y0", c.dram["d_y"][0:8, :, 0:256], reads=["d_y"])
    c.dump("srun", s_run[:, :, :], reads=[s_run])


def l0a_state_exchange(c):
    p, pool = c.p, c.pool
    ag_si = c.scratch("ag_si", [128, NH * 128], BF16)
    ag_so = c.scratch("ag_so", [NCORES, 128, NH * 128], BF16)
    ag_li = c.scratch("ag_li", [1, NH], F32)
    ag_lo = c.scratch("ag_lo", [NCORES, NH], F32)
    p.dma("dsy", ag_si, c.s_run[:, :, :].rearrange("q h t -> q (h t)"),
          reads=[c.s_run], writes=["ag_si"], tag="st_agsi")
    p.dma("dsy", ag_li.rearrange("o h -> (o h)").unsqueeze(1), c.mamba["logd"][:, 0:1],
          reads=[c.mamba["logd"]], writes=["ag_li"], tag="st_agli")
    p.emit("dcc", lambda e: e.collective_compute(
               "AllGather", ALU.bypass, replica_groups=[list(range(NCORES))],
               ins=[c.dram["ag_si"].opt()], outs=[c.dram["ag_so"].opt()]),
           reads=["ag_si"], writes=["ag_so"], tag="ag_state")
    p.emit("dcc", lambda e: e.collective_compute(
               "AllGather", ALU.bypass, replica_groups=[list(range(NCORES))],
               ins=[c.dram["ag_li"].opt()], outs=[c.dram["ag_lo"].opt()]),
           reads=["ag_li"], writes=["ag_lo"], tag="ag_logd")
    ld = pool.alloc("ld_all", [NH, NCORES], F32)
    p.dma("dsy", ld[:, :], c.dram["ag_lo"].transpose([1, 0]), reads=["ag_lo"],
          writes=[ld], tag="ld_aglo", allow_slow_non_contiguous=True)
    pmB = load_cast(c, "pmB", c.io["prevmask"].broadcast_to([NH, NCORES]), [NH, NCORES],
                    F32, q="dsy")
    suf = pool.alloc("suf", [NH, NCORES], F32)
    coef = pool.alloc("coef", [NH, NCORES], F32)
    p.emit("vec", lambda e: e.tensor_tensor(ld[:, :], ld[:, :], pmB[:, :], ALU.mult),
           reads=[ld, pmB], writes=[ld], tag="ldm")
    p.emit("vec", lambda e: e.memset(suf[:, NCORES - 1:NCORES], 0.0), writes=[suf], tag="suf0")
    for j in range(NCORES - 2, -1, -1):
        p.emit("vec", (lambda j2: lambda e: e.tensor_add(
                   suf[:, j2:j2 + 1], suf[:, j2 + 1:j2 + 2], ld[:, j2 + 1:j2 + 2]))(j),
               reads=[suf, ld], writes=[suf], tag="sufj")
    p.emit("sc", lambda e: e.activation(coef[:, :], suf[:, :], AF.Exp),
           reads=[suf], writes=[coef], tag="coefe")
    p.emit("vec", lambda e: e.tensor_tensor(coef[:, :], coef[:, :], pmB[:, :], ALU.mult),
           reads=[coef, pmB], writes=[coef], tag="coefm")
    d_coef = c.scratch("d_coef", [NH, NCORES], F32)
    p.dma("dsy", c.dram["d_coef"], coef[:, :], reads=[coef], writes=["d_coef"], tag="st_coef")
    coefB = pool.alloc("coefB", [128, NH, NCORES], F32)
    p.dma("dsy", coefB[:, :, :],
          c.dram["d_coef"].rearrange("h k -> (h k)").unsqueeze(0)
          .broadcast_to([128, NH * NCORES]).rearrange("q (h k) -> q h k", h=NH),
          reads=["d_coef"], writes=[coefB], tag="coefB")
    h_in = pool.alloc("h_in", [128, NH, 128], BF16)
    tmp = pool.alloc("hin_tmp", [128, NH, 128], BF16)
    stg = [pool.alloc(f"aghst{i}", [128, NH, 128], BF16) for i in range(2)]
    p.emit("vec", lambda e: e.memset(h_in[:, :, :], 0.0), writes=[h_in], tag="hin0")
    for j in range(NCORES):
        sj = stg[j % 2]
        p.dma("dsy", sj[:, :, :],
              c.dram["ag_so"][j].rearrange("q (h t) -> q h t", h=NH),
              reads=["ag_so"], writes=[sj], tag="ld_agst")
        p.emit("vec", (lambda sj2, j2: lambda e: e.tensor_tensor(
                   tmp[:, :, :], sj2[:, :, :],
                   coefB[:, :, j2:j2 + 1].broadcast_to([128, NH, 128]), ALU.mult))(sj, j),
               reads=[sj, coefB], writes=[tmp], tag="hin_m")
        p.emit("vec", lambda e: e.tensor_add(h_in[:, :, :], h_in[:, :, :], tmp[:, :, :]),
               reads=[h_in, tmp], writes=[h_in], tag="hin_a")
    for b in [tmp, suf, coef, coefB, ld, pmB] + stg:
        b.free()
    c.h_in = h_in
    c.dump("hin", h_in[:, :, :], reads=[h_in])


def l0a_pass2(c):
    p, pool, ps = c.p, c.pool, c.ps
    M = c.mamba
    s_cur = c.h_in
    expB = pool.alloc("expB2", [128, 32, QC], BF16)
    bcA = pool.alloc("bcA2", [128, 32 * QC], F32)
    cexp = pool.alloc("cexp2", [128, 32, QC], BF16)
    slin = [pool.alloc(f"slin{i}", [128, NH, 128], BF16) for i in range(2)]
    yio = [pool.alloc(f"yio{i}", [128, 32, QC], BF16) for i in range(2)]
    bcA3 = bcA[:, :].rearrange("q (h t) -> q h t", h=32)
    for ci in range(NCHUNK):
        t0 = ci * QC
        sl = slin[ci % 2]
        p.dma("dsy", sl[:, :, :],
              c.dram["d_sloc"][ci].rearrange("q (h t) -> q h t", h=NH),
              reads=["d_sloc"], writes=[sl], tag="ld_sloc")
        for hf in range(2):
            hc0 = hf * 32
            rowsrc = c.dram["d_acum"][ci, hc0:hc0 + 32, :].rearrange("h t -> (h t)").unsqueeze(0)
            p.dma("dsy", bcA[:, :], rowsrc.broadcast_to([128, 32 * QC]),
                  reads=["d_acum"], writes=[bcA], tag="bcA2")
            p.emit("sc", lambda e: e.activation(expB[:, :, :], bcA3, AF.Exp),
                   reads=[bcA], writes=[expB], tag="expB2")
            p.emit("vec", (lambda ci2: lambda e: e.tensor_tensor(cexp[:, :, :],
                       c.c_convs[:, ci2, :].unsqueeze(1).broadcast_to([128, 32, QC]),
                       expB[:, :, :], ALU.mult))(ci),
                   reads=[c.c_convs, expB], writes=[cexp], tag="cexp2")
            yb = yio[hf]
            p.dma("dsy", yb[:, :, :],
                  c.dram["d_y"][hc0:hc0 + 32, :, t0:t0 + QC].transpose([1, 0, 2]),
                  reads=["d_y", (("ystate", ci, hf))], writes=[yb], tag="ld_y2")
            banks = [ps.take() for _ in range(8)]
            for h in range(32):
                b = banks[h // 4]
                c.mm(b.f32[0:128, (h % 4) * 128:(h % 4 + 1) * 128],
                     s_cur[:, hc0 + h, :], cexp[:, h, :], True, True,
                     reads=[s_cur, cexp], writes=[b], tag="ysmm")
            for bi, b in enumerate(banks):
                p.emit("vec", (lambda b2, bi2, yb2: lambda e: e.tensor_tensor(
                           yb2[:, bi2 * 4:(bi2 + 1) * 4, :], b2.f32[0:128, :],
                           yb2[:, bi2 * 4:(bi2 + 1) * 4, :], ALU.add))(b, bi, yb),
                       reads=[b, yb], writes=[yb], tag="ys_ev")
            p.dma("dsy", c.dram["d_y"][hc0:hc0 + 32, :, t0:t0 + QC].transpose([1, 0, 2]),
                  yb[:, :, :], reads=[yb], writes=["d_y", ("ystate", ci, hf)], tag="st_y2")
        p.emit("vec", (lambda ci2: lambda e: e.tensor_tensor(s_cur[:, :, :], s_cur[:, :, :],
                   c.dtotB[:, ci2, :].unsqueeze(2).broadcast_to([128, NH, 128]),
                   ALU.mult))(ci), reads=[s_cur, c.dtotB], writes=[s_cur], tag="sc_m")
        p.emit("vec", (lambda sl2: lambda e: e.tensor_add(s_cur[:, :, :], s_cur[:, :, :],
                   sl2[:, :, :]))(sl), reads=[s_cur, sl], writes=[s_cur], tag="sc_a")
    for b in [expB, bcA, cexp] + slin + yio:
        b.free()
    c.h_in.free()
    c.s_run.free()
    c.aendB.free()
    c.dtotB.free()
    c.c_convs.free()


def l0a_proj_z(c):
    p, pool, ps = c.p, c.pool, c.ps
    c.d_z = c.scratch("d_z", [64, 128, TOK], BF16)
    wview = c.io["L0_attn_W_in"].rearrange("(kc q) m -> q kc m", q=128)
    wring = [pool.alloc(f"winz{i}", [128, 32, 128], BF16) for i in range(2)]
    zstg = [pool.alloc(f"zstg{i}", [128, TOK], BF16) for i in range(2)]
    for zi in range(64):
        m0 = zi * 128
        wb = wring[zi % 2]
        sb = zstg[zi % 2]
        p.dma("dgp", wb[:, :, :], wview[:, :, m0:m0 + 128], writes=[wb], tag="ldWinZ")
        for (n0, ncw) in ((3, 512), (515, 512)):
            bank = ps.take()
            for kc in range(32):
                c.mm(bank.f32[0:128, 0:ncw], wb[:, kc, :], c.lnx[:, kc, n0:n0 + ncw],
                     start=(kc == 0), stop=(kc == 31), reads=[wb, c.lnx], writes=[bank],
                     tag="projZ")
            p.emit("sc", (lambda b, sb2, n0b: lambda e: e.copy(
                       sb2[:, n0b - 3:n0b - 3 + 512], b.f32[0:128, 0:512]))(bank, sb, n0),
                   reads=[bank], writes=[sb], tag="z_evac")
        p.dma("dsy", c.dram["d_z"][zi], sb[:, :], reads=[sb], writes=["d_z"], tag="st_z")
    return wring + zstg


def l0a_out(c):
    p, pool, ps = c.p, c.pool, c.ps
    c.lnx.free()
    c.d_att = c.scratch("d_att", [32, 128, TOK], BF16)
    ynorm = pool.alloc("ynorm", [128, 64, TOK], BF16)
    zrow = [pool.alloc(f"zrow{i}", [128, TOK], BF16) for i in range(2)]
    yrow = [pool.alloc(f"yrow{i}", [128, TOK], BF16) for i in range(2)]
    sqs = pool.alloc("sqs", [128, TOK], BF16)
    stat_banks = [ps.take(), ps.take()]
    for fc in range(64):
        zr, yr = zrow[fc % 2], yrow[fc % 2]
        p.dma("dsy", yr[:, :], c.dram["d_y"][fc], reads=["d_y"], writes=[yr], tag="ld_yD")
        p.dma("dsy", zr[:, :], c.dram["d_z"][fc], reads=["d_z"], writes=[zr], tag="ld_zD")
        p.emit("sc", (lambda zr2: lambda e: e.activation(zr2[:, :], zr2[:, :], AF.Silu))(zr),
               reads=[zr], writes=[zr], tag="siluz")
        p.emit("vec", (lambda yr2, zr2, fc2: lambda e: e.tensor_tensor(
                   ynorm[:, fc2, :], yr2[:, :], zr2[:, :], ALU.mult))(yr, zr, fc),
               reads=[yr, zr], writes=[ynorm], tag="yz")
        p.emit("sc", (lambda fc2: lambda e: e.activation(sqs[:, :], ynorm[:, fc2, :],
                   AF.Square))(fc), reads=[ynorm], writes=[sqs], tag="ysq")
        for ni, b in enumerate(stat_banks):
            c.mm(b.f32[0:1, 0:512], c.ones_bf[:, 0:1], sqs[:, ni * 512:(ni + 1) * 512],
                 start=(fc == 0), stop=(fc == 63), reads=[c.ones_bf, sqs], writes=[b],
                 tag="ysqmm")
    rrow = pool.alloc("rrow", [1, TOK], F32)
    for ni, b in enumerate(stat_banks):
        p.emit("sc", (lambda b2, ni2: lambda e: e.activation(
                   rrow[0:1, ni2 * 512:(ni2 + 1) * 512], b2.f32[0:1, 0:512], AF.Ln,
                   scale=1.0 / DSSM, bias=c.eps_col[0:1, 0:1]))(b, ni),
               reads=[b, c.eps_col], writes=[rrow], tag="lnms")
    p.emit("sc", lambda e: e.activation(rrow[0:1, :], rrow[0:1, :], AF.Exp, scale=-0.5),
           reads=[rrow], writes=[rrow], tag="rsrow")
    d_rs = c.scratch("d_rs", [1, TOK], F32)
    p.dma("dsy", c.dram["d_rs"], rrow[0:1, :], reads=[rrow], writes=["d_rs"], tag="st_rs")
    rsB = pool.alloc("rsB", [128, TOK], F32)
    p.dma("dsy", rsB[:, :], c.dram["d_rs"].rearrange("o t -> (o t)").unsqueeze(0)
          .broadcast_to([128, TOK]), reads=["d_rs"], writes=[rsB], tag="rsB")
    for hb in range(2):
        p.emit("vec", (lambda h0: lambda e: e.tensor_tensor(
                   ynorm[:, h0:h0 + 32, :], ynorm[:, h0:h0 + 32, :],
                   rsB[:, :].unsqueeze(1).broadcast_to([128, 32, TOK]), ALU.mult))(hb * 32),
               reads=[ynorm, rsB], writes=[ynorm], tag="yrs")
    nwT = c.mamba["nwT"]
    for fc in range(64):
        p.emit("vec", (lambda fc2: lambda e: e.tensor_scalar(
                   ynorm[:, fc2, :], ynorm[:, fc2, :], nwT[:, 0, fc2:fc2 + 1], None,
                   ALU.mult))(fc), reads=[ynorm, nwT], writes=[ynorm], tag="ynw")
    woview = c.io["L0_attn_W_out"].rearrange("(kc q) m -> q kc m", q=128)
    woring = [pool.alloc(f"wor{i}", [128, 64, 128], BF16) for i in range(2)]
    astg = [pool.alloc(f"astg{i}", [128, TOK], BF16) for i in range(2)]
    for oi in range(32):
        wb, sb = woring[oi % 2], astg[oi % 2]
        p.dma("dgp", wb[:, :, :], woview[:, :, oi * 128:(oi + 1) * 128], writes=[wb],
              tag="ldWout")
        for (n0, ncw) in ((0, 512), (512, 512)):
            bank = ps.take()
            for kc in range(64):
                c.mm(bank.f32[0:128, 0:ncw], wb[:, kc, :], ynorm[:, kc, n0:n0 + ncw],
                     start=(kc == 0), stop=(kc == 63), reads=[wb, ynorm], writes=[bank],
                     tag="woutmm")
            p.emit("sc", (lambda b, sb2, n0b: lambda e: e.copy(
                       sb2[:, n0b:n0b + 512], b.f32[0:128, 0:512]))(bank, sb, n0),
                   reads=[bank], writes=[sb], tag="att_evac")
        p.dma("dsy", c.dram["d_att"][oi], sb[:, :], reads=[sb], writes=["d_att"], tag="st_att")
    for b in zrow + yrow + [sqs, rrow, rsB, ynorm] + woring + astg:
        b.free()
    c.dump("attfm", c.dram["d_att"][0:4], reads=["d_att"])


def sandwich_add(c, d_blk, x_src, out_name, final=False):
    p, pool, ps = c.p, c.pool, c.ps
    xnew = None if final else pool.alloc(out_name, [128, NCHUNK, H], BF16)
    aslab = [pool.alloc(f"{out_name}_as{i}", [128, 32, 128], BF16) for i in range(2)]
    atok = [pool.alloc(f"{out_name}_at{i}", [128, H], BF16) for i in range(2)]
    xp = [pool.alloc(f"{out_name}_xp{i}", [128, H], BF16) for i in range(2)]
    fin = [pool.alloc(f"{out_name}_fin{i}", [128, H], F32) for i in range(2)] if final else []
    sq = pool.alloc(f"{out_name}_sq", [128, H], BF16)
    st = pool.alloc(f"{out_name}_st", [128, 8], F32)
    for tc in range(8):
        asl, at = aslab[tc % 2], atok[tc % 2]
        p.dma("dsy", asl[:, :, :],
              c.dram[d_blk][:, :, tc * 128:(tc + 1) * 128].transpose([1, 0, 2]),
              reads=[d_blk], writes=[asl], tag="ld_blk")
        for g in range(4):
            srcs = [asl[:, g * 8 + j, :] for j in range(8)]
            transpose_tile_group(c, srcs, at[:, g * 1024:(g + 1) * 1024], at, reads=[asl])
        ln_tile(c, at[:, :], at[:, :], sq, st, xreads=[at], hwrites=[at])
        if isinstance(x_src, str):
            xpt = xp[tc % 2]
            p.dma("dsy", xpt[:, :], c.dram[x_src][tc], reads=[x_src], writes=[xpt], tag="ld_xp")
        else:
            xpt = None
        if final:
            fb = fin[tc % 2]
            if xpt is not None:
                p.emit("vec", (lambda at2, xp2, fb2: lambda e: e.tensor_add(
                           fb2[:, :], xp2[:, :], at2[:, :]))(at, xpt, fb),
                       reads=[at, xpt], writes=[fb], tag="resadd_f")
            else:
                p.emit("vec", (lambda at2, fb2, tc2: lambda e: e.tensor_add(
                           fb2[:, :], x_src[:, tc2, :], at2[:, :]))(at, fb, tc),
                       reads=[at, x_src], writes=[fb], tag="resadd_f")
            p.dma("dsy", c.out_ap[tc * 128:(tc + 1) * 128, :], fb[:, :], reads=[fb],
                  writes=[c.out_ap], tag="st_out")
        else:
            if xpt is not None:
                p.emit("vec", (lambda at2, xp2, tc2: lambda e: e.tensor_add(
                           xnew[:, tc2, :], xp2[:, :], at2[:, :]))(at, xpt, tc),
                       reads=[at, xpt], writes=[xnew], tag="resadd")
            else:
                p.emit("vec", (lambda at2, tc2: lambda e: e.tensor_add(
                           xnew[:, tc2, :], x_src[:, tc2, :], at2[:, :]))(at, tc),
                       reads=[at, x_src], writes=[xnew], tag="resadd")
    for b in aslab + atok + xp + fin + [sq, st]:
        b.free()
    if final:
        c.out_written = True
    return xnew


def load_transposed_stream(c, name, chunks, cols, out_dtype=BF16):
    p, ps = c.p, c.ps
    rc = len(chunks)
    cc = cols // 128
    out = c.pool.alloc(name + "_T", [128, cc, rc * 128], out_dtype)
    ring = [c.pool.alloc(f"{name}_n{i}", [128, cols], out_dtype) for i in range(2)]
    for ri in range(rc):
        nb = ring[ri % 2]
        ch = chunks[ri]
        if isinstance(ch, tuple):
            p.dma("dgp", nb[0:64, :], ch[0], writes=[nb], tag=f"ld_{name}")
            p.dma("dgp", nb[64:128, :], ch[1], reads=[nb], writes=[nb], tag=f"ld_{name}b")
        else:
            p.dma("dgp", nb[:, :], ch, writes=[nb], tag=f"ld_{name}")
        for cg in range(-(-cc // 8)):
            gw = min(8, cc - cg * 8)
            bank = ps.take()
            for j in range(gw):
                ci = cg * 8 + j
                c.transpose(bank.bf16[0:128, j * 128:(j + 1) * 128],
                            nb[:, ci * 128:(ci + 1) * 128], c.eye_bf[:, :],
                            reads=[nb, c.eye_bf], writes=[bank], tag=f"tr_{name}")
            dst = out[:, cg * 8:cg * 8 + gw, ri * 128:(ri + 1) * 128]
            src = bank.bf16[:, 0:gw * 128]
            p.emit("sc", (lambda d, s: lambda e: e.copy(d, s))(dst, src),
                   reads=[bank], writes=[out], tag=f"ev_{name}")
    for b in ring:
        b.free()
    return out


def bload(c, name, src_view, shape, dtype=F32):
    return load_cast(c, name, src_view, shape, dtype, q="dsy" if dtype == F32 else "dgp")


def ln_transpose_store(c, xt_ap, fm_buf, t0, reads):
    for g in range(4):
        srcs = [xt_ap[:, (g * 8 + j) * 128:(g * 8 + j + 1) * 128] for j in range(8)]
        dst = fm_buf.t[:, g * 8:(g + 1) * 8, t0:t0 + 128]
        transpose_tile_group(c, srcs, dst, fm_buf, reads=reads)



def l0_ffn(c):
    p, pool, ps = c.p, c.pool, c.ps
    x1 = c.x_cur
    c.d_x1 = c.scratch("d_x1", [8, 128, H], BF16)
    h_fm = pool.alloc("hfm", [128, 32, TOK], BF16)
    htok = [pool.alloc(f"f0h{i}", [128, H], BF16) for i in range(2)]
    sq = pool.alloc("f0sq", [128, H], BF16)
    st = pool.alloc("f0st", [128, 8], F32)
    lv = pool.alloc("lv", [128, 8], F32)
    for tc in range(8):
        hb = htok[tc % 2]
        ln_tile(c, x1[:, tc, :], hb[:, :], sq, st, xreads=[x1], hwrites=[hb])
        p.emit("vec", (lambda t: lambda e: e.tensor_tensor(
                   lv[:, t:t + 1], st[:, 5:6], st[:, 7:8], ALU.mult))(tc),
               reads=[st], writes=[lv], tag="varh1")
        p.emit("vec", (lambda t: lambda e: e.tensor_tensor(
                   lv[:, t:t + 1], lv[:, t:t + 1], st[:, 7:8], ALU.mult))(tc),
               reads=[lv, st], writes=[lv], tag="varh2")
        p.emit("sc", (lambda t: lambda e: e.activation(
                   lv[:, t:t + 1], lv[:, t:t + 1], AF.Ln, bias=c.one_col[:, 0:1]))(tc),
               reads=[lv, c.one_col], writes=[lv], tag="lv")
        ln_transpose_store(c, hb, h_fm, tc * 128, reads=[hb])
        p.dma("dsy", c.dram["d_x1"][tc], x1[:, tc, :], reads=[x1], writes=["d_x1"],
              tag="sp_x1")
    x1.free()
    for b in htok + [sq, st]:
        b.free()
    lvs = pool.alloc("lvs", [128, 1], F32)
    p.emit("vec", lambda e: e.reduce_sum(lvs[:, 0:1], lv[:, :], AX),
           reads=[lv], writes=[lvs], tag="lvsum")
    lvrow = pool.alloc("lvrow", [1, 128], F32)
    tr_f32(c, lvs[:, 0:1], 128, 1, lvrow[0:1, :], lvrow, reads=[lvs])
    d_aglv = c.scratch("d_aglv", [1, 64], F32)
    d_aglvo = c.scratch("d_aglvo", [NCORES, 64], F32)
    lvtot = pool.alloc("lvtot", [1, 64], F32)
    p.emit("vec", lambda e: e.memset(lvtot[0:1, :], 0.0), writes=[lvtot], tag="lvt0")
    p.emit("vec", lambda e: e.reduce_sum(lvtot[0:1, 0:1], lvrow[0:1, :], AX),
           reads=[lvrow, lvtot], writes=[lvtot], tag="lvtot")
    p.dma("dsy", c.dram["d_aglv"], lvtot[0:1, :], reads=[lvtot], writes=["d_aglv"],
          tag="st_aglv")
    p.emit("dcc", lambda e: e.collective_compute(
               "AllGather", ALU.bypass, replica_groups=[list(range(NCORES))],
               ins=[c.dram["d_aglv"].opt()], outs=[c.dram["d_aglvo"].opt()]),
           reads=["d_aglv"], writes=["d_aglvo"], tag="ag_lv")
    lva = pool.alloc("lva", [1, NCORES], F32)
    p.dma("dsy", lva[0:1, :], c.dram["d_aglvo"][:, 0:1].transpose([1, 0]),
          reads=["d_aglvo"], writes=[lva], tag="ld_aglvo",
          allow_slow_non_contiguous=True)
    gi = pool.alloc("gi", [1, 1], F32)
    e6 = pool.alloc("e6", [1, 1], F32)
    p.emit("vec", lambda e: e.memset(e6[:, :], 1e-6), writes=[e6])
    p.emit("vec", lambda e: e.reduce_sum(gi[0:1, 0:1], lva[0:1, :], AX),
           reads=[lva], writes=[gi], tag="gsum")
    p.emit("vec", lambda e: e.tensor_scalar(gi[0:1, 0:1], gi[0:1, 0:1], 1.0 / S_FULL,
                                            1e-6, ALU.mult, ALU.add),
           reads=[gi], writes=[gi], tag="gmean")
    p.emit("vec", lambda e: e.reciprocal(gi[0:1, 0:1], gi[0:1, 0:1]),
           reads=[gi], writes=[gi], tag="ginv")
    girow = pool.alloc("girow", [1, 128], F32)
    p.emit("vec", lambda e: e.tensor_copy(girow[0:1, :],
               gi[0:1, 0:1].broadcast_to([1, 128])),
           reads=[gi], writes=[girow], tag="girow")
    d_gi = c.scratch("d_gi", [1, 128], F32)
    p.dma("dsy", c.dram["d_gi"], girow[0:1, :], reads=[girow], writes=["d_gi"],
          tag="st_gi")
    giB = pool.alloc("giB", [128, 1], F32)
    p.dma("dsy", giB[:, :], c.dram["d_gi"][:, 0:1].rearrange("a b -> (a b)").unsqueeze(0)
          .broadcast_to([128, 1]), reads=["d_gi"], writes=[giB], tag="giB")
    girow.free()
    c.dump("lva", lva[0:1, :], reads=[lva])
    for b in (lvs, lvrow, lvtot, lva, gi, e6):
        b.free()
    wu1T = load_transposed_stream(
        c, "wu1", [c.io["L0_ffn_Wu1"][ri * 128:(ri + 1) * 128, :] for ri in range(8)], H)
    bu1B = bload(c, "bu1B", c.io["L0_ffn_bu1"].broadcast_to([128, 1024]), [128, 1024])
    wu2B = bload(c, "wu2B", c.io["L0_ffn_Wu2"].broadcast_to([128, 1024]), [128, 1024])
    bu2B = bload(c, "bu2B", c.io["L0_ffn_bu2"].broadcast_to([128, 1]), [128, 1])
    learned = pool.alloc("learned", [128, 8], F32)
    mid = pool.alloc("mid", [128, 1024], F32)
    for tc in range(8):
        for nh in range(2):
            bank = ps.take()
            for kc in range(32):
                c.mm(bank.f32[0:128, 0:512], h_fm[:, kc, tc * 128:(tc + 1) * 128],
                     wu1T[:, kc, nh * 512:(nh + 1) * 512], start=(kc == 0), stop=(kc == 31),
                     reads=[h_fm, wu1T], writes=[bank], tag="midmm")
            p.emit("vec", (lambda b, nh2: lambda e: e.tensor_tensor(
                       mid[:, nh2 * 512:(nh2 + 1) * 512], b.f32[0:128, 0:512],
                       bu1B[:, nh2 * 512:(nh2 + 1) * 512], ALU.add))(bank, nh),
                   reads=[bank, bu1B], writes=[mid], tag="mid_b")
        p.emit("sc", lambda e: e.activation(mid[:, :], mid[:, :], AF.Gelu_apprx_tanh),
               reads=[mid], writes=[mid], tag="mid_g")
        p.emit("vec", lambda e: e.tensor_tensor(mid[:, :], mid[:, :], wu2B[:, :], ALU.mult),
               reads=[mid, wu2B], writes=[mid], tag="mid_w2")
        p.emit("vec", (lambda t: lambda e: e.reduce_sum(learned[:, t:t + 1], mid[:, :], AX))(tc),
               reads=[mid], writes=[learned], tag="mid_rs")
    p.emit("vec", lambda e: e.tensor_tensor(learned[:, :], learned[:, :],
               bu2B[:, 0:1].broadcast_to([128, 8]), ALU.add),
           reads=[learned, bu2B], writes=[learned], tag="mid_b2")
    for b in (wu1T, bu1B, wu2B, bu2B, mid):
        b.free()
    wqT = load_transposed(c, "wq", c.io["L0_ffn_Wq"], 64, H)
    wrT = load_transposed(c, "wr", c.io["L0_ffn_Wr"], 48, 64)
    akT = load_transposed(c, "ak", c.io["L0_ffn_aux_keys"], 1024, 64)
    qT = pool.alloc("qT", [64, TOK], BF16)
    rqT = pool.alloc("rqT", [48, TOK], BF16)
    rkT = pool.alloc("rkT", [48, 1024], BF16)
    for ts in range(2):
        bank = ps.take()
        for kc in range(32):
            c.mm(bank.f32[0:64, 0:512], wqT[:, kc, :], h_fm[:, kc, ts * 512:(ts + 1) * 512],
                 start=(kc == 0), stop=(kc == 31), reads=[wqT, h_fm], writes=[bank], tag="qTmm")
        p.emit("sc", (lambda b, t: lambda e: e.copy(qT[:, t * 512:(t + 1) * 512],
                   b.f32[0:64, 0:512]))(bank, ts), reads=[bank], writes=[qT], tag="qT_ev")
    for ts in range(2):
        bank = ps.take()
        c.mm(bank.f32[0:48, 0:512], wrT[:, 0, :], qT[0:64, ts * 512:(ts + 1) * 512],
             True, True, reads=[wrT, qT], writes=[bank], tag="rqmm")
        p.emit("sc", (lambda b, t: lambda e: e.copy(rqT[:, t * 512:(t + 1) * 512],
                   b.f32[0:48, 0:512]))(bank, ts), reads=[bank], writes=[rqT], tag="rq_ev")
        bank2 = ps.take()
        c.mm(bank2.f32[0:48, 0:512], wrT[:, 0, :], akT[0:64, 0, ts * 512:(ts + 1) * 512],
             True, True, reads=[wrT, akT], writes=[bank2], tag="rkmm")
        p.emit("sc", (lambda b, t: lambda e: e.copy(rkT[:, t * 512:(t + 1) * 512],
                   b.f32[0:48, 0:512]))(bank2, ts), reads=[bank2], writes=[rkT], tag="rk_ev")
    wqT.free()
    h_fm.free()
    avT = load_transposed_stream(
        c, "av", [c.io["L0_ffn_aux_values"][ri * 128:(ri + 1) * 128, :] for ri in range(8)], H)
    wvdT = load_transposed_stream(
        c, "wvd", [c.io["L0_ffn_Wvd"][ri * 128:(ri + 1) * 128, :] for ri in range(2)], H)
    av2 = pool.alloc("av2", [128, 8, 256], BF16)
    for sc8 in range(8):
        bank = ps.take()
        for kc in range(32):
            c.mm(bank.f32[0:128, 0:256], avT[:, kc, sc8 * 128:(sc8 + 1) * 128],
                 wvdT[:, kc, 0:256], start=(kc == 0), stop=(kc == 31),
                 reads=[avT, wvdT], writes=[bank], tag="av2mm")
        p.emit("sc", (lambda b, s: lambda e: e.copy(av2[:, s, :], b.f32[0:128, 0:256]))(
                   bank, sc8), reads=[bank], writes=[av2], tag="av2_ev")
    avT.free()
    wvdT.free()
    wvuT = load_transposed_stream(
        c, "wvu", [c.io["L0_ffn_Wvu"][ri * 128:(ri + 1) * 128, :] for ri in range(32)], 256)
    logB = bload(c, "logB", c.io["L0_ffn_log_rel"].broadcast_to([128, 1024]), [128, 1024])
    gw1B = bload(c, "gw1B", c.io["L0_ffn_gate_w1"].broadcast_to([128, 1]), [128, 1])
    gbB = bload(c, "gbB", c.io["L0_ffn_gate_bias"].broadcast_to([128, 1]), [128, 1])
    aux2T = pool.alloc("aux2T", [128, 2, TOK], BF16)
    gatec = pool.alloc("gatec", [128, 8], F32)
    scr = pool.alloc("scr", [128, 1024], F32)
    s2 = pool.alloc("s2", [128, 1024], F32)
    top8 = pool.alloc("top8", [128, 8], F32)
    msk = pool.alloc("msk", [128, 1024], F32)
    w_bf = pool.alloc("w_bf", [128, 1024], BF16)
    wT8 = pool.alloc("wT8", [128, 8, 128], BF16)
    cols = pool.alloc("f4c", [128, 6], F32)
    SQ48 = 1.0 / np.sqrt(48.0)
    for tc in range(8):
        t0 = tc * 128
        for ns in range(2):
            bank = ps.take()
            c.mm(bank.f32[0:128, 0:512], rqT[0:48, t0:t0 + 128],
                 rkT[0:48, ns * 512:(ns + 1) * 512], True, True,
                 reads=[rqT, rkT], writes=[bank], tag="rsmm")
            p.emit("vec", (lambda b, n: lambda e: e.tensor_scalar(
                       scr[:, n * 512:(n + 1) * 512], b.f32[0:128, 0:512], SQ48, None,
                       ALU.mult))(bank, ns), reads=[bank], writes=[scr], tag="rs_ev")
            bank2 = ps.take()
            c.mm(bank2.f32[0:128, 0:512], qT[0:64, t0:t0 + 128],
                 akT[0:64, 0, ns * 512:(ns + 1) * 512], True, True,
                 reads=[qT, akT], writes=[bank2], tag="qkmm")
            p.emit("vec", (lambda b, n: lambda e: e.tensor_scalar(
                       s2[:, n * 512:(n + 1) * 512], b.f32[0:128, 0:512], 0.125, None,
                       ALU.mult))(bank2, ns), reads=[bank2], writes=[s2], tag="qk_ev")
        p.emit("vec", lambda e: e.tensor_add(scr[:, :], scr[:, :], logB[:, :]),
               reads=[scr, logB], writes=[scr], tag="rs_lg")
        p.emit("vec", lambda e: e.tensor_add(s2[:, :], s2[:, :], logB[:, :]),
               reads=[s2, logB], writes=[s2], tag="qk_lg")
        p.emit("vec", lambda e: e.max(top8[:, :], scr[:, :]),
               reads=[scr], writes=[top8], tag="top8")
        p.emit("vec", lambda e: e.tensor_scalar(msk[:, :], scr[:, :], top8[:, 7:8], None,
                   ALU.is_ge), reads=[scr, top8], writes=[msk], tag="msk")
        p.emit("vec", lambda e: e.tensor_tensor(s2[:, :], s2[:, :], msk[:, :], ALU.mult),
               reads=[s2, msk], writes=[s2], tag="s2m1")
        p.emit("vec", lambda e: e.tensor_scalar(msk[:, :], msk[:, :], 1e4, 1e4,
                   ALU.mult, ALU.subtract), reads=[msk], writes=[msk], tag="s2m2")
        p.emit("vec", lambda e: e.tensor_add(s2[:, :], s2[:, :], msk[:, :]),
               reads=[s2, msk], writes=[s2], tag="s2m3")
        p.emit("vec", lambda e: e.reduce_max(cols[:, 0:1], s2[:, :], AX),
               reads=[s2], writes=[cols], tag="s2mx")
        p.emit("vec", lambda e: e.tensor_scalar(cols[:, 1:2], cols[:, 0:1], -1.0, None,
                   ALU.mult), reads=[cols], writes=[cols], tag="s2mxn")
        p.emit("sc", lambda e: e.activation(w_bf[:, :], s2[:, :], AF.Exp,
                   bias=cols[:, 1:2], accum_out=cols[:, 2:3]),
               reads=[s2, cols], writes=[w_bf, cols], tag="wexp")
        p.emit("vec", lambda e: e.reciprocal(cols[:, 3:4], cols[:, 2:3]),
               reads=[cols], writes=[cols], tag="winv")
        for g in range(1):
            bank = ps.take()
            for j in range(8):
                c.transpose(bank.bf16[0:128, j * 128:(j + 1) * 128],
                            w_bf[:, j * 128:(j + 1) * 128], c.eye_bf[:, :],
                            reads=[w_bf, c.eye_bf], writes=[bank], tag="wtr")
            p.emit("sc", (lambda b: lambda e: e.copy(wT8[:, :, :], b.bf16[:, 0:1024]))(bank),
                   reads=[bank], writes=[wT8], tag="wtr_ev")
        bank = ps.take()
        for sc8 in range(8):
            c.mm(bank.f32[0:128, 0:256], wT8[:, sc8, :], av2[:, sc8, :],
                 start=(sc8 == 0), stop=(sc8 == 7), reads=[wT8, av2], writes=[bank],
                 tag="aux2mm")
        a2b = pool.alloc("a2b", [128, 256], BF16)
        p.emit("vec", (lambda b: lambda e: e.tensor_scalar(a2b[:, :], b.f32[0:128, 0:256],
                   cols[:, 3:4], None, ALU.mult))(bank),
               reads=[bank, cols], writes=[a2b], tag="aux2_ev")
        bank2 = ps.take()
        for j in range(2):
            c.transpose(bank2.bf16[0:128, j * 128:(j + 1) * 128],
                        a2b[:, j * 128:(j + 1) * 128], c.eye_bf[:, :],
                        reads=[a2b, c.eye_bf], writes=[bank2], tag="a2tr")
        p.emit("sc", (lambda b, t: lambda e: e.copy(aux2T[:, :, t:t + 128],
                   b.bf16[:, 0:256]))(bank2, t0), reads=[bank2], writes=[aux2T], tag="a2t_ev")
        a2b.free()
        p.emit("sc", (lambda t: lambda e: e.activation(cols[:, 4:5], learned[:, t:t + 1],
                   AF.Sigmoid))(tc), reads=[learned], writes=[cols], tag="g_sig")
        p.emit("vec", (lambda t: lambda e: e.tensor_tensor(cols[:, 5:6], lv[:, t:t + 1],
                   giB[:, 0:1], ALU.mult))(tc), reads=[lv, giB], writes=[cols], tag="g_nv")
        p.emit("vec", lambda e: e.tensor_scalar(cols[:, 5:6], cols[:, 5:6], 0.5, None,
                   ALU.mult), reads=[cols], writes=[cols], tag="g_nv2")
        p.emit("vec", lambda e: e.tensor_scalar(cols[:, 4:5], cols[:, 4:5], 2.5, None,
                   ALU.mult), reads=[cols], writes=[cols], tag="g_s2")
        p.emit("vec", lambda e: e.tensor_add(cols[:, 4:5], cols[:, 4:5], cols[:, 5:6]),
               reads=[cols], writes=[cols], tag="g_u")
        p.emit("vec", lambda e: e.tensor_scalar(cols[:, 4:5], cols[:, 4:5], 5.0, 0.0,
                   ALU.min, ALU.max), reads=[cols], writes=[cols], tag="g_clip")
        p.emit("vec", lambda e: e.tensor_scalar(cols[:, 4:5], cols[:, 4:5], 0.5,
                   1.0 / 1.5, ALU.subtract, ALU.mult), reads=[cols], writes=[cols], tag="g_un")
        p.emit("vec", lambda e: e.tensor_scalar(cols[:, 4:5], cols[:, 4:5], 1.0, 0.0,
                   ALU.min, ALU.max), reads=[cols], writes=[cols], tag="g_un2")
        p.emit("vec", lambda e: e.tensor_scalar(cols[:, 4:5], cols[:, 4:5], gw1B[:, 0:1],
                   gbB[:, 0:1], ALU.mult, ALU.add), reads=[cols, gw1B, gbB], writes=[cols],
               tag="g_lin")
        p.emit("sc", (lambda t: lambda e: e.activation(gatec[:, t:t + 1], cols[:, 4:5],
                   AF.Sigmoid))(tc), reads=[cols], writes=[gatec], tag="g_sig2")
        p.emit("vec", (lambda t: lambda e: e.tensor_scalar(cols[:, 5:6],
                   gatec[:, t:t + 1], 0.05, None, ALU.is_ge))(tc),
               reads=[gatec], writes=[cols], tag="g_ge")
        p.emit("vec", (lambda t: lambda e: e.tensor_tensor(gatec[:, t:t + 1],
                   gatec[:, t:t + 1], cols[:, 5:6], ALU.mult))(tc),
               reads=[gatec, cols], writes=[gatec], tag="g_m")
    c.dump("gate", gatec[:, :], reads=[gatec])
    c.dump("lv", lv[:, :], reads=[lv])
    c.dump("learned", learned[:, :], reads=[learned])
    c.dump("giB", giB[:, :], reads=[giB])
    for b in (scr, s2, top8, msk, w_bf, wT8, cols, qT, rqT, rkT, wrT, akT, av2, logB,
              gw1B, gbB, giB, lv, learned):
        b.free()
    c.d_x2 = c.scratch("d_x2", [8, 128, H], BF16)
    lnx2 = c.lnx2_fm = pool.alloc("lnx2", [128, 32, TOK], BF16)
    sq = pool.alloc("f5sq", [128, H], BF16)
    st = pool.alloc("f5st", [128, 8], F32)
    a3 = pool.alloc("a3", [128, H], F32)
    hrec = pool.alloc("hrec", [128, H], F32)
    x1t = [pool.alloc(f"x1t{i}", [128, H], BF16) for i in range(2)]
    x2t = pool.alloc("x2t", [128, H], BF16)
    for tc in range(8):
        t0 = tc * 128
        xb = x1t[tc % 2]
        p.dma("dsy", xb[:, :], c.dram["d_x1"][tc], reads=["d_x1"], writes=[xb], tag="ld_x1")
        for ms in range(8):
            bank = ps.take()
            for c2 in range(2):
                c.mm(bank.f32[0:128, 0:512], aux2T[:, c2, t0:t0 + 128],
                     wvuT[:, c2, ms * 512:(ms + 1) * 512], start=(c2 == 0), stop=(c2 == 1),
                     reads=[aux2T, wvuT], writes=[bank], tag="aux3mm")
            p.emit("vec", (lambda b, m, t: lambda e: e.tensor_scalar(
                       a3[:, m * 512:(m + 1) * 512], b.f32[0:128, 0:512],
                       gatec[:, t:t + 1], None, ALU.mult))(bank, ms, tc),
                   reads=[bank, gatec], writes=[a3], tag="aux3_ev")
        ln_tile(c, xb[:, :], hrec[:, :], sq, st, xreads=[xb], hwrites=[hrec])
        p.emit("vec", lambda e: e.tensor_add(a3[:, :], a3[:, :], hrec[:, :]),
               reads=[a3, hrec], writes=[a3], tag="ffn_out")
        ln_tile(c, a3[:, :], hrec[:, :], sq, st, xreads=[a3], hwrites=[hrec])
        p.emit("vec", (lambda x: lambda e: e.tensor_add(x2t[:, :], x[:, :], hrec[:, :]))(xb),
               reads=[xb, hrec], writes=[x2t], tag="x2add")
        p.dma("dsy", c.dram["d_x2"][tc], x2t[:, :], reads=[x2t], writes=["d_x2"], tag="st_x2")
        ln_tile(c, x2t[:, :], hrec[:, :], sq, st, xreads=[x2t], hwrites=[hrec])
        p.emit("vec", lambda e: e.tensor_copy(x2t[:, :], hrec[:, :]),
               reads=[hrec], writes=[x2t], tag="lnx2c")
        ln_transpose_store(c, x2t, lnx2, t0, reads=[x2t])
    for b in [aux2T, wvuT, gatec, a3, hrec, x2t, sq, st] + x1t:
        b.free()
    c.x_cur = None
    c.final_dram = "d_x2"



def l1_attn(c):
    p, pool, ps = c.p, c.pool, c.ps
    lnx2 = c.lnx2_fm
    DV = 1024
    bq_nat = bload(c, "bq_nat", c.io["L1_attn_bqkv"].rearrange("o (g p) -> (o g) p", p=128),
                   [24, 128])
    bqC = pool.alloc("bqC", [128, 24], F32)
    tr_f32(c, bq_nat[:, :], 24, 128, bqC[:, :], bqC, reads=[bq_nat])
    bq_nat.free()
    bvB = bload(c, "bvB", c.io["L1_attn_bqkv"][:, 2048:3072]
                .rearrange("a b -> (a b)").unsqueeze(0).broadcast_to([128, 1024]),
                [128, 1024])
    qfm = pool.alloc("qfm", [128, 8, TOK], BF16)
    kfm = pool.alloc("kfm", [128, 8, TOK], BF16)
    vtk = pool.alloc("vtk", [128, 8, DV], BF16)
    wview = c.io["L1_attn_Wqkv"].rearrange("(kc q) m -> q kc m", q=128)
    wring = [pool.alloc(f"wqk{i}", [128, 32, 256], BF16) for i in range(2)]
    for sl in range(8):
        wb = wring[sl % 2]
        p.dma("dgp", wb[:, :, :], wview[:, :, sl * 256:(sl + 1) * 256], writes=[wb],
              tag="ldWqk")
        for ds in range(2):
            gd = sl * 2 + ds
            dst, dc = (qfm, gd) if gd < 8 else (kfm, gd - 8)
            for ts in range(2):
                bank = ps.take()
                for kc in range(32):
                    c.mm(bank.f32[0:128, 0:512], wb[:, kc, ds * 128:(ds + 1) * 128],
                         lnx2[:, kc, ts * 512:(ts + 1) * 512], start=(kc == 0),
                         stop=(kc == 31), reads=[wb, lnx2], writes=[bank], tag="qkmm")
                p.emit("vec", (lambda b, d2, dc2, t2, g2: lambda e: e.tensor_scalar(
                           d2[:, dc2, t2 * 512:(t2 + 1) * 512], b.f32[0:128, 0:512],
                           bqC[:, g2:g2 + 1], None, ALU.add))(bank, dst, dc, ts, gd),
                       reads=[bank, bqC], writes=[dst], tag="qk_ev")
    for b in wring:
        b.free()
    vring = [pool.alloc(f"wv{i}", [128, 32, 512], BF16) for i in range(2)]
    for sl in range(2):
        wb = vring[sl % 2]
        p.dma("dgp", wb[:, :, :], wview[:, :, 2048 + sl * 512:2048 + (sl + 1) * 512],
              writes=[wb], tag="ldWv")
        for tc in range(8):
            bank = ps.take()
            for kc in range(32):
                c.mm(bank.f32[0:128, 0:512], lnx2[:, kc, tc * 128:(tc + 1) * 128],
                     wb[:, kc, :], start=(kc == 0), stop=(kc == 31),
                     reads=[wb, lnx2], writes=[bank], tag="vmm")
            p.emit("vec", (lambda b, t2, s2_: lambda e: e.tensor_tensor(
                       vtk[:, t2, s2_ * 512:(s2_ + 1) * 512], b.f32[0:128, 0:512],
                       bvB[:, s2_ * 512:(s2_ + 1) * 512], ALU.add))(bank, tc, sl),
                   reads=[bank, bvB], writes=[vtk], tag="v_ev")
    for b in vring + [bqC, bvB]:
        b.free()
    lnx2.free()
    c.dump("q0", qfm[:, :, 0:128], reads=[qfm])
    d_kf = c.scratch("d_kf", [128, 8 * TOK], BF16)
    d_vt = c.scratch("d_vt", [128, 8 * DV], BF16)
    d_kfo = c.scratch("d_kfo", [NCORES, 128, 8 * TOK], BF16)
    d_vto = c.scratch("d_vto", [NCORES, 128, 8 * DV], BF16)
    p.dma("dsy", c.dram["d_kf"], kfm[:, :, :].rearrange("p a t -> p (a t)"),
          reads=[kfm], writes=["d_kf"], tag="st_kf")
    p.dma("dsy", c.dram["d_vt"], vtk[:, :, :].rearrange("p a t -> p (a t)"),
          reads=[vtk], writes=["d_vt"], tag="st_vt")
    p.emit("dcc", lambda e: e.collective_compute(
               "AllGather", ALU.bypass, replica_groups=[list(range(NCORES))],
               ins=[c.dram["d_kf"].opt()], outs=[c.dram["d_kfo"].opt()]),
           reads=["d_kf"], writes=["d_kfo"], tag="ag_k")
    p.emit("dcc", lambda e: e.collective_compute(
               "AllGather", ALU.bypass, replica_groups=[list(range(NCORES))],
               ins=[c.dram["d_vt"].opt()], outs=[c.dram["d_vto"].opt()]),
           reads=["d_vt"], writes=["d_vto"], tag="ag_v")
    kfm.free()
    vtk.free()
    acc = pool.alloc("acc", [128, 8, DV], F32)
    mO = pool.alloc("mO", [128, 8], F32)
    lO = pool.alloc("lO", [128, 8], F32)
    p.emit("vec", lambda e: e.memset(acc[:, :, :], 0.0), writes=[acc])
    p.emit("vec", lambda e: e.memset(mO[:, :], -1e30), writes=[mO])
    p.emit("vec", lambda e: e.memset(lO[:, :], 0.0), writes=[lO])
    kslab = [pool.alloc(f"ksl{i}", [128, 8, 512], BF16) for i in range(2)]
    vslab = [pool.alloc(f"vsl{i}", [128, 4, DV], BF16) for i in range(2)]
    pbf = pool.alloc("pbf", [128, 512], BF16)
    pts = pool.alloc("pts", [128, 512], BF16)
    fcol = pool.alloc("fcol", [128, 4], F32)
    kview = c.dram["d_kfo"].rearrange("c p (a t) -> c p a t", a=8)
    vview = c.dram["d_vto"].rearrange("c p (a d) -> c p a d", a=8)
    ISQ = 1.0 / 32.0
    for s in range(16):
        cr, toff = s // 2, (s % 2) * 512
        kb, vb = kslab[s % 2], vslab[s % 2]
        p.dma("dsy", kb[:, :, :], kview[cr][:, :, toff:toff + 512],
              reads=["d_kfo"], writes=[kb], tag="ld_ksl")
        p.dma("dsy", vb[:, :, :], vview[cr][:, (s % 2) * 4:(s % 2) * 4 + 4, :],
              reads=["d_vto"], writes=[vb], tag="ld_vsl")
        for tc in range(8):
            bank = ps.take()
            for dc in range(8):
                c.mm(bank.f32[0:128, 0:512], qfm[:, dc, tc * 128:(tc + 1) * 128],
                     kb[:, dc, :], start=(dc == 0), stop=(dc == 7),
                     reads=[qfm, kb], writes=[bank], tag="smm")
            p.emit("vec", (lambda b: lambda e: e.reduce_max(fcol[:, 0:1],
                       b.f32[0:128, 0:512], AX))(bank), reads=[bank], writes=[fcol],
                   tag="smax")
            p.emit("vec", (lambda t: lambda e: e.tensor_tensor(fcol[:, 1:2],
                       mO[:, t:t + 1], fcol[:, 0:1], ALU.max))(tc),
                   reads=[mO, fcol], writes=[fcol], tag="mnew")
            p.emit("vec", lambda e: e.tensor_scalar(fcol[:, 2:3], fcol[:, 1:2],
                       -ISQ, None, ALU.mult), reads=[fcol], writes=[fcol], tag="mneg")
            p.emit("vec", (lambda t: lambda e: e.tensor_tensor(fcol[:, 3:4],
                       mO[:, t:t + 1], fcol[:, 1:2], ALU.subtract))(tc),
                   reads=[mO, fcol], writes=[fcol], tag="mdif")
            p.emit("sc", lambda e: e.activation(fcol[:, 3:4], fcol[:, 3:4], AF.Exp,
                       scale=ISQ), reads=[fcol], writes=[fcol], tag="corr")
            p.emit("vec", (lambda t: lambda e: e.tensor_copy(mO[:, t:t + 1],
                       fcol[:, 1:2]))(tc), reads=[fcol], writes=[mO], tag="mupd")
            p.emit("sc", (lambda b: lambda e: e.activation(pbf[:, :], b.f32[0:128, 0:512],
                       AF.Exp, scale=ISQ, bias=fcol[:, 2:3], accum_out=fcol[:, 0:1]))(bank),
                   reads=[bank, fcol], writes=[pbf, fcol], tag="pexp")
            p.emit("vec", (lambda t: lambda e: e.scalar_tensor_tensor(
                       lO[:, t:t + 1], lO[:, t:t + 1], fcol[:, 3:4], fcol[:, 0:1],
                       ALU.mult, ALU.add))(tc), reads=[lO, fcol], writes=[lO], tag="lupd")
            bank2 = ps.take()
            for j in range(4):
                c.transpose(bank2.bf16[0:128, j * 128:(j + 1) * 128],
                            pbf[:, j * 128:(j + 1) * 128], c.eye_bf[:, :],
                            reads=[pbf, c.eye_bf], writes=[bank2], tag="ptr")
            p.emit("sc", (lambda b: lambda e: e.copy(pts[:, :], b.bf16[:, 0:512]))(bank2),
                   reads=[bank2], writes=[pts], tag="ptr_ev")
            for hh in range(2):
                bank3 = ps.take()
                for kc in range(4):
                    c.mm(bank3.f32[0:128, 0:512], pts[:, kc * 128:(kc + 1) * 128],
                         vb[:, kc, hh * 512:(hh + 1) * 512], start=(kc == 0), stop=(kc == 3),
                         reads=[pts, vb], writes=[bank3], tag="avmm")
                p.emit("vec", (lambda b, t, h2: lambda e: e.scalar_tensor_tensor(
                           acc[:, t, h2 * 512:(h2 + 1) * 512],
                           acc[:, t, h2 * 512:(h2 + 1) * 512], fcol[:, 3:4],
                           b.f32[0:128, 0:512], ALU.mult, ALU.add))(bank3, tc, hh),
                       reads=[bank3, acc, fcol], writes=[acc], tag="accupd")
    for b in kslab + vslab + [pbf, pts]:
        b.free()
    qfm.free()
    linv = pool.alloc("linv", [128, 8], F32)
    p.emit("vec", lambda e: e.reciprocal(linv[:, :], lO[:, :]),
           reads=[lO], writes=[linv], tag="linv")
    aot = pool.alloc("aot", [128, 8, DV], BF16)
    aob = pool.alloc("aob", [128, DV], BF16)
    for tc in range(8):
        p.emit("vec", (lambda t: lambda e: e.tensor_scalar(aob[:, :], acc[:, t, :],
                   linv[:, t:t + 1], None, ALU.mult))(tc),
               reads=[acc, linv], writes=[aob], tag="aodiv")
        bank = ps.take()
        for j in range(8):
            c.transpose(bank.bf16[0:128, j * 128:(j + 1) * 128],
                        aob[:, j * 128:(j + 1) * 128], c.eye_bf[:, :],
                        reads=[aob, c.eye_bf], writes=[bank], tag="aotr")
        p.emit("sc", (lambda b, t: lambda e: e.copy(aot[:, :, t * 128:(t + 1) * 128],
                   b.bf16[:, 0:1024].rearrange("p (a t) -> p a t", a=8)))(bank, tc),
               reads=[bank], writes=[aot], tag="aotr_ev")
    c.dump("ao0", aot[:, :, 0:128], reads=[aot])
    acc.free()
    mO.free()
    lO.free()
    linv.free()
    aob.free()
    boB = bload(c, "boB", c.io["L1_attn_bo"].broadcast_to([128, H]), [128, H])
    po = pool.alloc("po", [128, 8, H], BF16)
    woview = c.io["L1_attn_Wo"].rearrange("(kc q) m -> q kc m", q=128)
    woring = [pool.alloc(f"wo{i}", [128, 8, 512], BF16) for i in range(2)]
    for ms in range(8):
        wb = woring[ms % 2]
        p.dma("dgp", wb[:, :, :], woview[:, :, ms * 512:(ms + 1) * 512], writes=[wb],
              tag="ldWo")
        for tc in range(8):
            bank = ps.take()
            for dc in range(8):
                c.mm(bank.f32[0:128, 0:512], aot[:, dc, tc * 128:(tc + 1) * 128],
                     wb[:, dc, :], start=(dc == 0), stop=(dc == 7),
                     reads=[aot, wb], writes=[bank], tag="pomm")
            p.emit("vec", (lambda b, t, m: lambda e: e.tensor_tensor(
                       po[:, t, m * 512:(m + 1) * 512], b.f32[0:128, 0:512],
                       boB[:, m * 512:(m + 1) * 512], ALU.add))(bank, tc, ms),
                   reads=[bank, boB], writes=[po], tag="po_ev")
    for b in woring + [boB, aot]:
        b.free()
    c.d_x3 = c.scratch("d_x3", [8, 128, H], BF16)
    lnx3 = c.lnx3_fm = pool.alloc("lnx3", [128, 32, TOK], BF16)
    sq = pool.alloc("a3sq", [128, H], BF16)
    st = pool.alloc("a3st", [128, 8], F32)
    xb = [pool.alloc(f"a3x{i}", [128, H], BF16) for i in range(2)]
    hb = pool.alloc("a3h", [128, H], F32)
    x3t = pool.alloc("x3t", [128, H], BF16)
    for tc in range(8):
        xt = xb[tc % 2]
        p.dma("dsy", xt[:, :], c.dram["d_x2"][tc], reads=["d_x2"], writes=[xt], tag="ld_x2")
        ln_tile(c, po[:, tc, :], hb[:, :], sq, st, xreads=[po], hwrites=[hb])
        p.emit("vec", (lambda x: lambda e: e.tensor_add(x3t[:, :], x[:, :], hb[:, :]))(xt),
               reads=[xt, hb], writes=[x3t], tag="x3add")
        p.dma("dsy", c.dram["d_x3"][tc], x3t[:, :], reads=[x3t], writes=["d_x3"], tag="st_x3")
        ln_tile(c, x3t[:, :], hb[:, :], sq, st, xreads=[x3t], hwrites=[hb])
        p.emit("vec", lambda e: e.tensor_copy(x3t[:, :], hb[:, :]),
               reads=[hb], writes=[x3t], tag="lnx3c")
        ln_transpose_store(c, x3t, lnx3, tc * 128, reads=[x3t])
    for b in xb + [po, sq, st, hb, x3t]:
        b.free()
    c.final_dram = "d_x3"



def l1_ffn(c):
    import os as _os
    lvl = int(_os.environ.get("MK_L1F", "9"))
    p, pool, ps = c.p, c.pool, c.ps
    lnx3 = c.lnx3_fm
    wr1T = load_transposed(c, "wr1", c.io["L1_ffn_Wr"], 64, H)
    G = pool.alloc("G", [128, 8, 64], F32)
    sc64 = pool.alloc("sc64", [128, 64], F32)
    t8 = pool.alloc("t8", [128, 8], F32)
    t8b = pool.alloc("t8b", [128, 8], F32)
    srep = pool.alloc("srep", [128, 64], F32)
    gcol = pool.alloc("gcol", [128, 4], F32)
    if lvl < 1:
        p.emit("vec", lambda e: e.memset(G[:, :, :], 0.0), writes=[G], tag="G0")
    for tc in range(8 if lvl >= 1 else 0):
        bank = ps.take()
        for kc in range(32):
            c.mm(bank.f32[0:128, 0:64], lnx3[:, kc, tc * 128:(tc + 1) * 128],
                 wr1T[:, kc, :], start=(kc == 0), stop=(kc == 31),
                 reads=[lnx3, wr1T], writes=[bank], tag="rscmm")
        p.emit("vec", (lambda b: lambda e: e.tensor_copy(sc64[:, :], b.f32[0:128, 0:64]))(
                   bank), reads=[bank], writes=[sc64], tag="rsc_ev")
        p.emit("vec", lambda e: e.max(t8[:, :], sc64[:, :]), reads=[sc64], writes=[t8],
               tag="t8")
        p.emit("vec", lambda e: e.match_replace(srep[:, :], t8[:, :], sc64[:, :], -3e38),
               reads=[sc64, t8], writes=[srep], tag="mrep")
        p.emit("vec", lambda e: e.max(t8b[:, :], srep[:, :]), reads=[srep], writes=[t8b],
               tag="t8b")
        p.emit("vec", lambda e: e.tensor_scalar(srep[:, :], sc64[:, :], t8b[:, 7:8], None,
                   ALU.is_ge), reads=[sc64, t8b], writes=[srep], tag="gmsk")
        p.emit("vec", lambda e: e.tensor_scalar(gcol[:, 0:1], t8[:, 0:1], -1.0, None,
                   ALU.mult), reads=[t8], writes=[gcol], tag="gmx")
        p.emit("sc", lambda e: e.activation(sc64[:, :], sc64[:, :], AF.Exp,
                   bias=gcol[:, 0:1]), reads=[sc64, gcol], writes=[sc64], tag="gexp")
        p.emit("vec", lambda e: e.tensor_tensor(sc64[:, :], sc64[:, :], srep[:, :],
                   ALU.mult), reads=[sc64, srep], writes=[sc64], tag="gexpm")
        p.emit("vec", lambda e: e.reduce_sum(gcol[:, 1:2], sc64[:, :], AX),
               reads=[sc64], writes=[gcol], tag="gsum")
        p.emit("vec", lambda e: e.reciprocal(gcol[:, 2:3], gcol[:, 1:2]),
               reads=[gcol], writes=[gcol], tag="ginv")
        p.emit("vec", (lambda t: lambda e: e.tensor_scalar(G[:, t, :], sc64[:, :],
                   gcol[:, 2:3], None, ALU.mult))(tc), reads=[sc64, gcol], writes=[G],
               tag="gnorm")
    c.dump("G", G[:, :, :], reads=[G])
    for b in (wr1T, sc64, t8, t8b, srep, gcol):
        b.free()
    a_ap = c.io["L1_ffn_A"].rearrange("k r d -> r k d")
    at = load_transposed_stream(
        c, "a2", [(a_ap[2 * ri], a_ap[2 * ri + 1]) for ri in range(8)], H)
    hgTe = pool.alloc("hgTe", [64, 8, TOK], BF16)
    hgTo = pool.alloc("hgTo", [64, 8, TOK], BF16)
    hgb = pool.alloc("hgb", [128, 1024], BF16)
    hgs = pool.alloc("hgs", [128, 1024], BF16)
    for tc in range(8 if lvl >= 2 else 0):
        t0 = tc * 128
        for nh in range(2):
            bank = ps.take()
            for kc in range(32):
                c.mm(bank.f32[0:128, 0:512], lnx3[:, kc, t0:t0 + 128],
                     at[:, kc, nh * 512:(nh + 1) * 512], start=(kc == 0), stop=(kc == 31),
                     reads=[lnx3, at], writes=[bank], tag="hamm")
            p.emit("sc", (lambda b, n: lambda e: e.copy(
                       hgb[:, n * 512:(n + 1) * 512], b.f32[0:128, 0:512]))(bank, nh),
                   reads=[bank], writes=[hgb], tag="hg_ev")
        p.emit("vec", (lambda t: lambda e: e.tensor_tensor(
                   hgb[:, :].rearrange("p (r k) -> p r k", r=16),
                   hgb[:, :].rearrange("p (r k) -> p r k", r=16),
                   G[:, t, :].unsqueeze(1).broadcast_to([128, 16, 64]),
                   ALU.mult))(tc),
               reads=[hgb, G], writes=[hgb], tag="hg_g")
        bank2 = ps.take()
        for j in range(8):
            c.transpose(bank2.bf16[0:128, j * 128:(j + 1) * 128],
                        hgb[:, j * 128:(j + 1) * 128], c.eye_bf[:, :],
                        reads=[hgb, c.eye_bf], writes=[bank2], tag="hgtr")
        p.emit("sc", (lambda b, t: lambda e: e.copy(
                   hgTe[0:64, :, t:t + 128], b.bf16[0:64, 0:1024].rearrange(
                       "p (a t) -> p a t", a=8)))(bank2, t0),
               reads=[bank2], writes=[hgTe], tag="hgtr_ev")
        p.emit("sc", (lambda b: lambda e: e.copy(
                   hgs[64:128, 0:1024], b.bf16[64:128, 0:1024]))(bank2),
               reads=[bank2], writes=[hgs], tag="hgtr_st")
        p.dma("dsy", hgTo[0:64, :, t0:t0 + 128],
              hgs[64:128, 0:1024].rearrange("p (a t) -> p a t", a=8),
              reads=[hgs], writes=[hgTo], tag="hgtr_dma")
    at.free()
    hgb.free()
    hgs.free()
    G.free()
    lnx3.free()
    moe = pool.alloc("moe", [128, 8, H], BF16)
    if lvl < 4:
        p.emit("vec", lambda e: e.memset(moe[:, :, :], 0.0), writes=[moe], tag="moe0")
    bview = c.io["L1_ffn_B"]
    bslab = [pool.alloc(f"bsl{i}", [64, 512, 16], BF16) for i in range(2)]
    bbr = [pool.alloc(f"bbr{i}", [64, 16, 512], BF16) for i in range(2)]
    for ms in range(8 if lvl >= 3 else 0):
        bb, br = bslab[ms % 2], bbr[ms % 2]
        src = bview[:, ms * 512:(ms + 1) * 512, :]
        p.dma("dgp", bb[0:64, :, :], src, writes=[bb], tag="ldB1")
        for r in range(16):
            p.emit("sc", (lambda b2, b3, r2: lambda e: e.copy(
                       b3[0:64, r2, :], b2[0:64, :, r2]))(bb, br, r),
                   reads=[bb], writes=[br], tag="brr")
        for tc in range(8 if lvl >= 4 else 0):
            bank = ps.take()
            for r in range(16):
                hsrc = hgTe if r % 2 == 0 else hgTo
                c.mm(bank.f32[0:128, 0:512],
                     hsrc[0:64, r // 2, tc * 128:(tc + 1) * 128],
                     br[0:64, r, :], start=(r == 0), stop=(r == 15),
                     reads=[hsrc, br], writes=[bank], tag="moemm")
            p.emit("vec", (lambda b, t, m: lambda e: e.tensor_scalar(
                       moe[:, t, m * 512:(m + 1) * 512], b.f32[0:128, 0:512], 2.0, None,
                       ALU.mult))(bank, tc, ms), reads=[bank], writes=[moe], tag="moe_ev")
    c.dump("moe", moe[:, :, :], reads=[moe])
    c.dump("hgTe", hgTe[0:64, :, 0:128], reads=[hgTe])
    c.dump("hgTo", hgTo[0:64, :, 0:128], reads=[hgTo])
    for b in bslab + bbr + [hgTe, hgTo]:
        b.free()
    sq = pool.alloc("m5sq", [128, H], BF16)
    st = pool.alloc("m5st", [128, 8], F32)
    xb = [pool.alloc(f"m5x{i}", [128, H], BF16) for i in range(2)]
    hb = pool.alloc("m5h", [128, H], F32)
    fo = [pool.alloc(f"m5f{i}", [128, H], F32) for i in range(2)]
    for tc in range(8):
        xt, ft = xb[tc % 2], fo[tc % 2]
        p.dma("dsy", xt[:, :], c.dram["d_x3"][tc], reads=["d_x3"], writes=[xt], tag="ld_x3")
        ln_tile(c, moe[:, tc, :], hb[:, :], sq, st, xreads=[moe], hwrites=[hb])
        p.emit("vec", (lambda x, f: lambda e: e.tensor_add(f[:, :], x[:, :], hb[:, :]))(
                   xt, ft), reads=[xt, hb], writes=[ft], tag="outadd")
        p.dma("dsy", c.out_ap[tc * 128:(tc + 1) * 128, :], ft[:, :], reads=[ft],
              writes=[c.out_ap], tag="st_out")
    for b in xb + fo + [moe, sq, st, hb]:
        b.free()
    c.out_written = True
